# version 15
# baseline (speedup 1.0000x reference)
import contextlib
import math
import numpy as np

import concourse.bass as bass
import concourse.mybir as mybir
from concourse.bass_utils import run_bass_kernel_spmd

F32 = mybir.dt.float32
F32R = mybir.dt.float32r
BF16 = mybir.dt.bfloat16
AF = mybir.ActivationFunctionType
ALU = mybir.AluOpType
AX = mybir.AxisListType

NCORES = 8
B = 16
BPC = B // NCORES
S = 2048
D = 1024
H = 16
DK = 64
NCH = D // 128
NKC = S // 128
GW = 3968
NEGC = -12.0



def _bucket_onehot():
    rel = np.arange(-2047, 2048)
    ret = (rel > 0).astype(np.int32) * 16
    n = np.abs(rel)
    n_safe = np.maximum(n, 1).astype(np.float32)
    val = 8 + (np.log(n_safe / np.float32(8.0)) / np.log(16.0) * 8).astype(np.int32)
    val = np.minimum(val, 15)
    bk = ret + np.where(n < 8, n, val)
    oh = np.zeros((32, 4096), np.float32)
    oh[bk, np.arange(4095)] = 1.0
    return oh


def _rbc_const():
    r = np.zeros((8, 128), np.float32)
    for e in range(8):
        r[e, e * 16:(e + 1) * 16] = 1.0
    return r



DMA_SEMS = ("s_ld", "s_gp", "s_out")


class Sched:
    def __init__(self, nc, sems):
        self.nc = nc
        self.sems = sems
        self.count = {k: 0 for k in sems}
        self.items = []
        self.flush_base = {k: 0 for k in sems}
        self.sv_count = 0

    def w(self, s):
        return (s, self.count[s])

    def on(self, eng, fn, waits=(), incs=()):
        self.items.append((eng, list(waits), fn, list(incs)))
        for s, n in incs:
            self.count[s] += n

    def flush(self):
        waited = {s: set() for s in DMA_SEMS}
        for _, waits, _, _ in self.items:
            for s, v in waits:
                if s in waited and v > 0:
                    waited[s].add(v)
        for s in DMA_SEMS:
            if self.count[s] > self.flush_base[s]:
                waited[s].add(self.count[s])
        per = {}
        for it in self.items:
            per.setdefault(it[0], []).append(it)
        self.items = []
        run = dict(self.flush_base)
        with self.nc.Block() as block:
            for engname, ops in per.items():
                def body(e, ops=ops, engname=engname):
                    last = {}
                    cert = None
                    for _, waits, fn, incs in ops:
                        for s, v in waits:
                            if v <= 0 or last.get(s, -1) >= v:
                                continue
                            e.wait_ge(self.sems[s], v)
                            last[s] = v
                        if engname == "vector" and cert is not None:
                            e.wait_ge(self.sems[cert[0]], cert[1])
                            last[cert[0]] = max(last.get(cert[0], 0), cert[1])
                        inst = fn(e)
                        if engname == "vector" and not incs:
                            self.sv_count += 1
                            inst = inst.then_inc(self.sems["s_sv"], 1)
                            cert = ("s_sv", self.sv_count)
                        for s, n in incs:
                            inst = inst.then_inc(self.sems[s], n)
                            run[s] += n
                            if engname == "vector":
                                cert = (s, run[s])
                        for s, n in incs:
                            if s in waited and run[s] in waited[s]:
                                e.wait_ge(self.sems[s], run[s])
                                last[s] = run[s]
                getattr(block, engname)(body)
        self.flush_base = dict(self.count)



def build():
    nc = bass.Bass("TRN2", target_bir_lowering=False)

    ap = lambda *a, **k: nc.dram_tensor(*a, **k).ap()
    x_in = ap("x", [BPC, S, D], F32, kind="ExternalInput")
    wq_in = ap("wq", [D, D], F32, kind="ExternalInput")
    wk_in = ap("wk", [D, D], F32, kind="ExternalInput")
    wv_in = ap("wv", [D, D], F32, kind="ExternalInput")
    wo_in = ap("wo", [D, D], F32, kind="ExternalInput")
    lng_in = ap("ln_g", [D], F32, kind="ExternalInput")
    lnb_in = ap("ln_b", [D], F32, kind="ExternalInput")
    rel_in = ap("rel_bias", [32, H], F32, kind="ExternalInput")
    bw_in = ap("base_wt", [D, D], F32, kind="ExternalInput")
    bb_in = ap("base_b", [D], F32, kind="ExternalInput")
    gw_in = ap("gate_wt", [D, 10], F32, kind="ExternalInput")
    la_in = ap("lora_at", [D, 128], F32, kind="ExternalInput")
    lb_in = ap("lora_bt", [128, D], F32, kind="ExternalInput")
    oh_in = ap("onehot", [32, 4096], F32, kind="ExternalInput")
    id_in = ap("ident", [128, 128], F32, kind="ExternalInput")
    rbc_in = ap("rbc", [8, 128], F32, kind="ExternalInput")
    y_out = ap("y", [BPC, S, D], F32, kind="ExternalOutput")

    rbd_hbm = ap("rbd_scr", [H, 4096], F32, kind="Internal")
    k_scr = ap("k_scr", [BPC, H, 128, S], BF16, kind="Internal")
    q_scr = ap("q_scr", [BPC, H, 128, S], BF16, kind="Internal")
    v_scr = ap("v_scr", [BPC, H, S, DK], F32, kind="Internal")

    with contextlib.ExitStack() as root:
        sb = lambda name, shape, dt=F32: root.enter_context(nc.sbuf_tensor(name, shape, dt))
        sems = {n: root.enter_context(nc.semaphore(n))
                for n in ["s_ld", "s_gp", "s_mm", "s_act", "s_dve", "s_out", "s_sv"]}
        sch = Sched(nc, sems)

        x1 = [sb(f"x1_{c}", [128, S]) for c in range(NCH)]
        ones = sb("ones", [128, 128])
        ident = sb("ident_sb", [128, 128])
        ident_r = sb("ident_r", [128, 128], F32R)
        negc = sb("negc", [128, 1])
        lng = sb("lng", [128, NCH])
        lnb = sb("lnb", [128, NCH])
        bbc = sb("bbc", [128, NCH])
        gate_sb = sb("gate_sb", [128, NCH * 10])
        la_sb = sb("la_sb", [128, D])
        lb_sb = sb("lb_sb", [128, D])
        rbc_sb = sb("rbc_sb", [8, 128])
        rows = sb("rows", [1, 2048])

        psA = root.enter_context(nc.psum_tensor("psA", [128, 1024], F32))
        psB = root.enter_context(nc.psum_tensor("psB", [128, 1024], F32))
        psC = root.enter_context(nc.psum_tensor("psC", [128, 1024], F32))
        psD = root.enter_context(nc.psum_tensor("psD", [128, 1024], F32))

        with contextlib.ExitStack() as pre:
            psb = lambda name, shape, dt=F32: pre.enter_context(nc.sbuf_tensor(name, shape, dt))
            oh_sb = psb("oh_sb", [32, 4096])
            rel_sb = psb("rel_sb", [32, H])
            rbdT = psb("rbdT", [16, 4096])

            def L(dst, src):
                return lambda e, d=dst, s=src: e.dma_start(d, s)
            sch.on("sync", L(ident[:, :], id_in), incs=[("s_ld", 16)])
            sch.on("sync", L(oh_sb[:, :], oh_in), incs=[("s_ld", 16)])
            sch.on("sync", L(rel_sb[:, :], rel_in), incs=[("s_ld", 16)])
            sch.on("sync", L(rbc_sb[:, :], rbc_in), incs=[("s_ld", 16)])
            for dst, src in [(lng, lng_in), (lnb, lnb_in), (bbc, bb_in)]:
                sch.on("sync",
                       (lambda e, d=dst, s=src:
                        e.dma_start(bass.AP(d, 0, [[NCH, 128], [1, NCH]]),
                                    bass.AP(s.tensor, 0, [[1, 128], [128, NCH]]),
                                    allow_slow_non_contiguous=True)),
                       incs=[("s_ld", 16)])
            sch.on("sync",
                   (lambda e: e.dma_start(
                       bass.AP(gate_sb, 0, [[NCH * 10, 128], [10, NCH], [1, 10]]),
                       bass.AP(gw_in.tensor, 0, [[10, 128], [1280, NCH], [1, 10]]))),
                   incs=[("s_ld", 16)])
            sch.on("sync",
                   (lambda e: e.dma_start(
                       bass.AP(la_sb, 0, [[D, 128], [128, NCH], [1, 128]]),
                       bass.AP(la_in.tensor, 0, [[128, 128], [16384, NCH], [1, 128]]))),
                   incs=[("s_ld", 16)])
            sch.on("sync", L(lb_sb[:, :], lb_in), incs=[("s_ld", 16)])
            sch.on("gpsimd", L(ident_r[:, :], id_in), incs=[("s_gp", 16)])
            sch.on("vector", lambda e: e.memset(ones[:, :], 1.0), incs=[("s_dve", 1)])
            sch.on("vector", lambda e: e.memset(negc[:, :], NEGC), incs=[("s_dve", 1)])
            wld = sch.w("s_ld")
            for g in range(8):
                half = g % 2
                sch.on("tensor",
                       (lambda e, g=g, half=half: e.matmul(
                           psA[0:16, half * 512:(half + 1) * 512],
                           rel_sb[:, :], oh_sb[:, g * 512:(g + 1) * 512],
                           start=True, stop=True)),
                       waits=[wld, sch.w("s_dve")] if g == 0 else [sch.w("s_dve")],
                       incs=[("s_mm", 1)])
                sch.on("vector",
                       (lambda e, g=g, half=half: e.tensor_copy(
                           rbdT[:, g * 512:(g + 1) * 512],
                           psA[0:16, half * 512:(half + 1) * 512])),
                       waits=[sch.w("s_mm")], incs=[("s_dve", 1)])
            sch.on("gpsimd", lambda e: e.dma_start(rbd_hbm, rbdT[:, :]),
                   waits=[sch.w("s_dve")], incs=[("s_gp", 16)])
            sch.flush()

        for seq in range(BPC):
            with contextlib.ExitStack() as sc1:
                s1sb = lambda name, shape: sc1.enter_context(nc.sbuf_tensor(f"{name}_s{seq}", shape, F32))
                normed_h = [sc1.enter_context(nc.sbuf_tensor(f"nrmh_{c}_s{seq}", [128, S], BF16))
                            for c in range(NCH)]
                normed_l = [sc1.enter_context(nc.sbuf_tensor(f"nrml_{c}_s{seq}", [128, S], BF16))
                            for c in range(NCH)]
                x_tm = [s1sb(f"xtm_{i}", [128, 1024]) for i in range(2)]
                tmp0 = s1sb("tmp0", [128, 512])
                xsq = [s1sb(f"xsq_{i}", [128, 512]) for i in range(2)]

                for tt in range(4):
                    t0 = tt * 512
                    tr_done = []
                    for tc in range(4):
                        waits = [("s_mm", tr_done[tc - 2])] if tc >= 2 else [sch.w("s_dve")]
                        sch.on("sync",
                               (lambda e, tc=tc, t0=t0: e.dma_start(
                                   x_tm[tc % 2][:, :],
                                   x_in[seq, t0 + tc * 128:t0 + (tc + 1) * 128, :])),
                               waits=waits, incs=[("s_ld", 16)])
                        wld = sch.w("s_ld")
                        for dc in range(NCH):
                            pst = [psA, psB, psC, psD][dc // 2]
                            half = dc % 2
                            sch.on("tensor",
                                   (lambda e, dc=dc, tc=tc, half=half, pst=pst: e.transpose(
                                       pst[:, half * 512 + tc * 128: half * 512 + (tc + 1) * 128],
                                       x_tm[tc % 2][:, dc * 128:(dc + 1) * 128], ident[:, :])),
                                   waits=([wld, sch.w("s_dve")] if dc == 0 else []),
                                   incs=([("s_mm", 1)] if dc == NCH - 1 else []))
                        tr_done.append(sch.count["s_mm"])
                    for dc in range(NCH):
                        pst = [psA, psB, psC, psD][dc // 2]
                        half = dc % 2
                        sch.on("vector",
                               (lambda e, dc=dc, half=half, t0=t0, pst=pst: e.tensor_copy(
                                   x1[dc][:, t0:t0 + 512], pst[:, half * 512:(half + 1) * 512])),
                               waits=[("s_mm", tr_done[3])], incs=[("s_dve", 1)])
                    for dc in range(NCH):
                        sch.on("tensor",
                               (lambda e, dc=dc, st=(dc == 0), t0=t0: e.matmul(
                                   psA[0:1, 0:512], ones[:, 0:1], x1[dc][:, t0:t0 + 512],
                                   start=st, stop=(dc == NCH - 1))),
                               waits=[sch.w("s_dve")],
                               incs=([("s_mm", 1)] if dc == NCH - 1 else []))
                    for dc in range(NCH):
                        sch.on("scalar",
                               (lambda e, dc=dc, t0=t0: e.activation(
                                   xsq[dc % 2][:, :], x1[dc][:, t0:t0 + 512], AF.Square)),
                               waits=[sch.w("s_dve"), sch.w("s_mm")],
                               incs=[("s_act", 1)])
                        sch.on("tensor",
                               (lambda e, dc=dc, st=(dc == 0): e.matmul(
                                   psA[0:1, 512:1024], ones[:, 0:1], xsq[dc % 2][:, :],
                                   start=st, stop=(dc == NCH - 1))),
                               waits=[sch.w("s_act")], incs=[("s_mm", 1)])
                    wmm = sch.w("s_mm")
                    sch.on("vector", lambda e: e.tensor_scalar(
                        rows[0:1, 0:512], psA[0:1, 0:512], 1.0 / D, None, ALU.mult),
                        waits=[wmm])
                    sch.on("vector", lambda e: e.tensor_scalar(
                        rows[0:1, 512:1024], psA[0:1, 512:1024], 1.0 / D, None, ALU.mult))
                    sch.on("vector", lambda e: e.tensor_tensor(
                        rows[0:1, 1024:1536], rows[0:1, 0:512], rows[0:1, 0:512], ALU.mult))
                    sch.on("vector", lambda e: e.tensor_tensor(
                        rows[0:1, 1536:2048], rows[0:1, 512:1024], rows[0:1, 1024:1536], ALU.subtract))
                    sch.on("vector", lambda e: e.tensor_scalar(
                        rows[0:1, 512:1024], rows[0:1, 1536:2048], 1e-6, None, ALU.add),
                        incs=[("s_dve", 1)])
                    sch.on("scalar", lambda e: e.activation(
                        rows[0:1, 1024:1536], rows[0:1, 512:1024], AF.Sqrt),
                        waits=[sch.w("s_dve")], incs=[("s_act", 1)])
                    sch.on("vector", lambda e: e.reciprocal(
                        rows[0:1, 1536:2048], rows[0:1, 1024:1536]),
                        waits=[sch.w("s_act")], incs=[("s_dve", 1)])
                    sch.on("tensor", lambda e: e.matmul(
                        psB[:, 0:512], ones[0:1, 0:128], rows[0:1, 0:512],
                        start=True, stop=True), waits=[sch.w("s_dve")])
                    sch.on("tensor", lambda e: e.matmul(
                        psB[:, 512:1024], ones[0:1, 0:128], rows[0:1, 1536:2048],
                        start=True, stop=True), incs=[("s_mm", 1)])
                    wmm = sch.w("s_mm")
                    for dc in range(NCH):
                        sch.on("vector",
                               (lambda e, dc=dc, t0=t0: e.tensor_tensor(
                                   tmp0[:, :], x1[dc][:, t0:t0 + 512], psB[:, 0:512], ALU.subtract)),
                               waits=[wmm] if dc == 0 else [])
                        sch.on("vector",
                               (lambda e: e.tensor_tensor(
                                   tmp0[:, :], tmp0[:, :], psB[:, 512:1024], ALU.mult)))
                        sch.on("vector",
                               (lambda e, dc=dc: e.tensor_scalar(
                                   tmp0[:, :], tmp0[:, :],
                                   lng[:, dc:dc + 1], lnb[:, dc:dc + 1], ALU.mult, ALU.add)))
                        sch.on("vector",
                               (lambda e, dc=dc, t0=t0: e.tensor_copy(
                                   normed_h[dc][:, t0:t0 + 512], tmp0[:, :])))
                        sch.on("vector",
                               (lambda e, dc=dc, t0=t0: e.tensor_tensor(
                                   normed_l[dc][:, t0:t0 + 512], tmp0[:, :],
                                   normed_h[dc][:, t0:t0 + 512], ALU.subtract)))
                        sch.on("vector",
                               (lambda e, dc=dc, t0=t0: e.tensor_scalar(
                                   x1[dc][:, t0:t0 + 512], x1[dc][:, t0:t0 + 512],
                                   2.0, None, ALU.mult)),
                               incs=[("s_dve", 1)])
                sch.flush()

                with contextlib.ExitStack() as sc2:
                    s2F = lambda name, shape: sc2.enter_context(nc.sbuf_tensor(f"{name}_s{seq}", shape, F32))
                    s2B = lambda name, shape: sc2.enter_context(nc.sbuf_tensor(f"{name}_s{seq}", shape, BF16))
                    wstg = [s2F(f"wstg{i}", [128, 1024]) for i in range(2)]
                    wqh = [s2B(f"wqh{i}", [128, 1024]) for i in range(2)]
                    wql = [s2B(f"wql{i}", [128, 1024]) for i in range(2)]
                    wkh = [s2B(f"wkh{i}", [128, 1024]) for i in range(2)]
                    wkl = [s2B(f"wkl{i}", [128, 1024]) for i in range(2)]
                    wvh = [s2B(f"wvh{i}", [128, 1024]) for i in range(2)]
                    wvl = [s2B(f"wvl{i}", [128, 1024]) for i in range(2)]
                    ksth = s2B("ksth", [128, 1024])
                    kstl = s2B("kstl", [128, 1024])
                    qsth = s2B("qsth", [128, 1024])
                    qstl = s2B("qstl", [128, 1024])
                    vst = s2F("vst", [128, 1024])

                    mm_done_pair = []
                    stg_free = [0, 0]
                    for pair in range(8):
                        pp = pair % 2
                        c0 = pair * 128
                        for wi, (hi_t, lo_t, srcw) in enumerate(
                                [(wqh[pp], wql[pp], wq_in), (wkh[pp], wkl[pp], wk_in),
                                 (wvh[pp], wvl[pp], wv_in)]):
                            stg = wstg[wi % 2]
                            dwaits = [("s_dve", stg_free[wi % 2])]
                            sch.on("sync",
                                   (lambda e, stg=stg, srcw=srcw, c0=c0: e.dma_start(
                                       bass.AP(stg, 0, [[1024, 128], [128, NCH], [1, 128]]),
                                       bass.AP(srcw.tensor, c0, [[D, 128], [128 * D, NCH], [1, 128]]))),
                                   waits=dwaits, incs=[("s_ld", 16)])
                            wld = sch.w("s_ld")
                            hwaits = [wld]
                            if pair >= 2:
                                hwaits.append(("s_mm", mm_done_pair[pair - 2]))
                            sch.on("vector",
                                   (lambda e, hi_t=hi_t, stg=stg: e.tensor_copy(hi_t[:, :], stg[:, :])),
                                   waits=hwaits)
                            sch.on("vector",
                                   (lambda e, hi_t=hi_t, lo_t=lo_t, stg=stg: e.tensor_tensor(
                                       lo_t[:, :], stg[:, :], hi_t[:, :], ALU.subtract)),
                                   incs=[("s_dve", 1)])
                            stg_free[wi % 2] = sch.count["s_dve"]
                        wsp = sch.w("s_dve")
                        for tt in range(4):
                            t0 = tt * 512
                            half = tt % 2
                            for ph in range(3):
                                wt_t = [wkh[pp], wkh[pp], wkl[pp]][ph]
                                nr = [normed_h, normed_l, normed_h][ph]
                                for dc in range(NCH):
                                    sch.on("tensor",
                                           (lambda e, dc=dc, half=half, wt_t=wt_t, nr=nr, t0=t0,
                                                   st=(ph == 0 and dc == 0),
                                                   sp=(ph == 2 and dc == NCH - 1): e.matmul(
                                               psC[:, half * 512:(half + 1) * 512],
                                               wt_t[:, dc * 128:(dc + 1) * 128],
                                               nr[dc][:, t0:t0 + 512],
                                               start=st, stop=sp)),
                                           waits=([wsp, sch.w("s_dve")] if (ph == 0 and dc == 0) else []),
                                           incs=([("s_mm", 1)] if (ph == 2 and dc == NCH - 1) else []))
                            sch.on("vector",
                                   (lambda e, half=half: e.tensor_copy(
                                       ksth[:, half * 512:(half + 1) * 512],
                                       psC[:, half * 512:(half + 1) * 512])),
                                   waits=[sch.w("s_mm"), sch.w("s_gp")])
                            sch.on("vector",
                                   (lambda e, half=half: e.tensor_tensor(
                                       kstl[:, half * 512:(half + 1) * 512],
                                       psC[:, half * 512:(half + 1) * 512],
                                       ksth[:, half * 512:(half + 1) * 512], ALU.subtract)),
                                   incs=[("s_dve", 1)])
                            for hh in range(2):
                                sch.on("gpsimd",
                                       (lambda e, hh=hh, half=half, pair=pair, t0=t0: e.dma_start(
                                           k_scr[seq, 2 * pair + hh, 0:64, t0:t0 + 512],
                                           ksth[hh * 64:(hh + 1) * 64, half * 512:(half + 1) * 512])),
                                       waits=[sch.w("s_dve")], incs=[("s_gp", 16)])
                                sch.on("gpsimd",
                                       (lambda e, hh=hh, half=half, pair=pair, t0=t0: e.dma_start(
                                           k_scr[seq, 2 * pair + hh, 64:128, t0:t0 + 512],
                                           kstl[hh * 64:(hh + 1) * 64, half * 512:(half + 1) * 512])),
                                       incs=[("s_gp", 16)])
                            for ph in range(3):
                                wt_t = [wqh[pp], wqh[pp], wql[pp]][ph]
                                nr = [normed_h, normed_l, normed_h][ph]
                                for dc in range(NCH):
                                    sch.on("tensor",
                                           (lambda e, dc=dc, half=half, wt_t=wt_t, nr=nr, t0=t0,
                                                   st=(ph == 0 and dc == 0),
                                                   sp=(ph == 2 and dc == NCH - 1): e.matmul(
                                               psD[:, half * 512:(half + 1) * 512],
                                               wt_t[:, dc * 128:(dc + 1) * 128],
                                               nr[dc][:, t0:t0 + 512],
                                               start=st, stop=sp)),
                                           waits=([sch.w("s_dve")] if (ph == 0 and dc == 0) else []),
                                           incs=([("s_mm", 1)] if (ph == 2 and dc == NCH - 1) else []))
                            sch.on("vector",
                                   (lambda e, half=half: e.tensor_copy(
                                       qsth[:, half * 512:(half + 1) * 512],
                                       psD[:, half * 512:(half + 1) * 512])),
                                   waits=[sch.w("s_mm"), sch.w("s_gp")])
                            sch.on("vector",
                                   (lambda e, half=half: e.tensor_tensor(
                                       qstl[:, half * 512:(half + 1) * 512],
                                       psD[:, half * 512:(half + 1) * 512],
                                       qsth[:, half * 512:(half + 1) * 512], ALU.subtract)),
                                   incs=[("s_dve", 1)])
                            for hh in range(2):
                                sch.on("gpsimd",
                                       (lambda e, hh=hh, half=half, pair=pair, t0=t0: e.dma_start(
                                           q_scr[seq, 2 * pair + hh, 0:64, t0:t0 + 512],
                                           qsth[hh * 64:(hh + 1) * 64, half * 512:(half + 1) * 512])),
                                       waits=[sch.w("s_dve")], incs=[("s_gp", 16)])
                                sch.on("gpsimd",
                                       (lambda e, hh=hh, half=half, pair=pair, t0=t0: e.dma_start(
                                           q_scr[seq, 2 * pair + hh, 64:128, t0:t0 + 512],
                                           qstl[hh * 64:(hh + 1) * 64, half * 512:(half + 1) * 512])),
                                       incs=[("s_gp", 16)])
                            for tc in range(4):
                                for ph in range(3):
                                    nr = [normed_h, normed_h, normed_l][ph]
                                    wt_t = [wvh[pp], wvl[pp], wvh[pp]][ph]
                                    for dc in range(NCH):
                                        sch.on("tensor",
                                               (lambda e, dc=dc, tc=tc, half=half, nr=nr, wt_t=wt_t, t0=t0,
                                                       st=(ph == 0 and dc == 0),
                                                       sp=(ph == 2 and dc == NCH - 1): e.matmul(
                                                   psA[:, half * 512 + tc * 128: half * 512 + (tc + 1) * 128],
                                                   nr[dc][:, t0 + tc * 128:t0 + (tc + 1) * 128],
                                                   wt_t[:, dc * 128:(dc + 1) * 128],
                                                   start=st, stop=sp)),
                                               waits=([sch.w("s_dve")] if (tc == 0 and ph == 0 and dc == 0) else []),
                                               incs=([("s_mm", 1)] if (ph == 2 and dc == NCH - 1) else []))
                            sch.on("vector",
                                   (lambda e, half=half: e.tensor_copy(
                                       vst[:, half * 512:(half + 1) * 512],
                                       psA[:, half * 512:(half + 1) * 512])),
                                   waits=[sch.w("s_mm"), sch.w("s_gp")], incs=[("s_dve", 1)])
                            for hh in range(2):
                                sch.on("gpsimd",
                                       (lambda e, hh=hh, half=half, pair=pair, t0=t0: e.dma_start(
                                           bass.AP(v_scr.tensor,
                                                   ((seq * H + 2 * pair + hh) * S + t0) * DK,
                                                   [[DK, 128], [128 * DK, 4], [1, DK]]),
                                           bass.AP(vst, half * 512 + hh * 64,
                                                   [[1024, 128], [128, 4], [1, DK]]))),
                                       waits=[sch.w("s_dve")], incs=[("s_gp", 16)])
                        mm_done_pair.append(sch.count["s_mm"])
                    sch.flush()

            with contextlib.ExitStack() as sc3:
                s3sb = lambda name, shape, dt=F32: sc3.enter_context(nc.sbuf_tensor(f"{name}_s{seq}", shape, dt))
                k_sb = [s3sb(f"k_sb{i}", [128, S], BF16) for i in range(4)]
                q_hd = [s3sb(f"qhd{i}", [128, S], BF16) for i in range(4)]
                q_l = [s3sb(f"ql{i}", [64, S], BF16) for i in range(4)]
                v_sb = [s3sb(f"v_sb{i}", [128, NKC * 130]) for i in range(2)]
                g_sb = [s3sb(f"g_sb{i}", [128, GW], F32R) for i in range(2)]
                p_sb = [s3sb(f"p_sb{i}", [128, 1024]) for i in range(2)]
                wo_sb = [s3sb("wo_sb0", [128, 1024])]
                wo_sb = [wo_sb[0], wo_sb[0]]
                ctx_sb = s3sb("ctx_sb", [128, 2048])
                scw = s3sb("scw", [128, 2048])

                for i in range(2):
                    sch.on("vector", (lambda e, i=i: e.memset(v_sb[i][:, :], 1.0)),
                           incs=[("s_dve", 1)])
                wmemset = sch.w("s_dve")

                def stream_pair(pair, wait_mm):
                    pp = pair % 2
                    for hh in range(2):
                        head = 2 * pair + hh
                        idx = pp * 2 + hh
                        sch.on("sync",
                               (lambda e, idx=idx, head=head: e.dma_start(
                                   k_sb[idx][:, :], k_scr[seq, head, :, :])),
                               waits=wait_mm, incs=[("s_ld", 16)])
                        sch.on("sync",
                               (lambda e, idx=idx, head=head: e.dma_start(
                                   q_hd[idx][0:64, :], q_scr[seq, head, 0:64, :])),
                               incs=[("s_ld", 16)])
                        sch.on("sync",
                               (lambda e, idx=idx, head=head: e.dma_start(
                                   q_hd[idx][64:128, :], q_scr[seq, head, 0:64, :])),
                               incs=[("s_ld", 16)])
                        sch.on("sync",
                               (lambda e, idx=idx, head=head: e.dma_start(
                                   q_l[idx][:, :], q_scr[seq, head, 64:128, :])),
                               incs=[("s_ld", 16)])
                        sch.on("sync",
                               (lambda e, pp=pp, hh=hh, head=head: e.dma_start(
                                   bass.AP(v_sb[pp], hh * 65, [[NKC * 130, 128], [130, NKC], [1, DK]]),
                                   bass.AP(v_scr.tensor, (seq * H + head) * S * DK,
                                           [[DK, 128], [128 * DK, NKC], [1, DK]]))),
                               incs=[("s_ld", 16)])

                def fetch_g(head, wait_mm):
                    slot = head % 2
                    sch.on("gpsimd",
                           (lambda e, slot=slot, head=head: e.dma_start(
                               g_sb[slot][:, :],
                               bass.AP(rbd_hbm.tensor, head * 4096, [[1, 128], [1, GW]]))),
                           waits=wait_mm, incs=[("s_gp", 16)])

                stream_pair(0, [wmemset])
                fetch_g(0, [])
                fetch_g(1, [])
                pair_mm_done = {}
                pair_kv_done = {}
                head_seed_done = {}

                pending_tail = [None, None]

                def make_tail(q0, hh, wpv):
                    dst = (ctx_sb[0:64, q0:q0 + 1024] if hh == 0
                           else scw[0:64, 1024:2048])

                    def tail_a():
                        sch.on("vector", lambda e: e.tensor_copy(
                            scw[64:65, 0:1024], psC[64:65, 0:1024]),
                            waits=[("s_mm", wpv)], incs=[("s_dve", 1)])
                        sch.on("vector",
                               (lambda e, dst=dst: e.tensor_copy(
                                   dst, psC[0:64, 0:1024])),
                               waits=([sch.w("s_gp")] if hh == 1 else []),
                               incs=[("s_dve", 1)])

                    def tail_b():
                        sch.on("vector", lambda e: e.reciprocal(
                            scw[64:65, 1024:2048], scw[64:65, 0:1024]),
                            waits=[sch.w("s_dve")],
                            incs=[("s_dve", 1)])
                        sch.on("tensor", lambda e: e.matmul(
                            psD[0:64, 0:512], ones[64:65, 0:64], scw[64:65, 1024:1536],
                            start=True, stop=True),
                            waits=[sch.w("s_dve")])
                        sch.on("tensor", lambda e: e.matmul(
                            psD[0:64, 512:1024], ones[64:65, 0:64], scw[64:65, 1536:2048],
                            start=True, stop=True), incs=[("s_mm", 1)])
                        sch.on("vector", lambda e: e.tensor_copy(
                            scw[0:64, 0:1024], psD[0:64, 0:1024]),
                            waits=[sch.w("s_mm")], incs=[("s_dve", 1)])
                        sch.on("vector",
                               (lambda e, dst=dst: e.tensor_tensor(
                                   dst, dst, scw[0:64, 0:1024], ALU.mult)),
                               incs=[("s_dve", 1)])
                        if hh == 1:
                            sch.on("gpsimd",
                                   (lambda e, q0=q0: e.dma_start(
                                       ctx_sb[64:128, q0:q0 + 1024], scw[0:64, 1024:2048])),
                                   waits=[sch.w("s_dve")], incs=[("s_gp", 16)])
                    return tail_a, tail_b

                for pair in range(8):
                    pp = pair % 2
                    wkv = sch.w("s_ld")
                    if pair < 7:
                        stream_pair(pair + 1,
                                    [("s_mm", pair_kv_done[pair - 1])] if pair >= 1 else [wmemset])
                    for hh in range(2):
                        head = 2 * pair + hh
                        wg = sch.w("s_gp")
                        exp_hist = []
                        pv_hist = []
                        for qt in range(2):
                            q0 = qt * 1024
                            for kc in range(NKC):
                                bank = [psA, psB][kc % 2]
                                w0a = 2047 + kc * 128 - q0
                                w0b = w0a - 512
                                waits = [wkv, wg]
                                idx = qt * NKC + kc
                                if idx >= 2:
                                    waits.append(("s_act", exp_hist[idx - 2]))
                                else:
                                    waits.append(sch.w("s_act"))
                                sch.on("tensor",
                                       (lambda e, bank=bank, w0a=w0a, gs=g_sb[head % 2]: e.matmul(
                                           bank[:, 0:512], ident_r[:, :],
                                           bass.AP(gs, w0a, [[GW, 128], [-1, 512]]),
                                           start=True, stop=False)),
                                       waits=waits)
                                sch.on("tensor",
                                       (lambda e, bank=bank, w0b=w0b, gs=g_sb[head % 2]: e.matmul(
                                           bank[:, 512:1024], ident_r[:, :],
                                           bass.AP(gs, w0b, [[GW, 128], [-1, 512]]),
                                           start=True, stop=False)))
                                khx = pp * 2 + hh
                                sch.on("tensor",
                                       (lambda e, bank=bank, kc=kc, khx=khx, q0=q0: e.matmul(
                                           bank[:, 0:512],
                                           k_sb[khx][:, kc * 128:(kc + 1) * 128],
                                           q_hd[khx][:, q0:q0 + 512],
                                           start=False, stop=False)))
                                sch.on("tensor",
                                       (lambda e, bank=bank, kc=kc, khx=khx, q0=q0: e.matmul(
                                           bank[:, 0:512],
                                           k_sb[khx][0:64, kc * 128:(kc + 1) * 128],
                                           q_l[khx][:, q0:q0 + 512],
                                           start=False, stop=True)))
                                sch.on("tensor",
                                       (lambda e, bank=bank, kc=kc, khx=khx, q0=q0: e.matmul(
                                           bank[:, 512:1024],
                                           k_sb[khx][:, kc * 128:(kc + 1) * 128],
                                           q_hd[khx][:, q0 + 512:q0 + 1024],
                                           start=False, stop=False)))
                                sch.on("tensor",
                                       (lambda e, bank=bank, kc=kc, khx=khx, q0=q0: e.matmul(
                                           bank[:, 512:1024],
                                           k_sb[khx][0:64, kc * 128:(kc + 1) * 128],
                                           q_l[khx][:, q0 + 512:q0 + 1024],
                                           start=False, stop=True)),
                                       incs=[("s_mm", 1)])
                                if head == 0 and qt == 0 and kc == 0:
                                    head_seed_done[head] = sch.count["s_mm"]
                                w_scores = sch.w("s_mm")
                                if kc == 0 and pending_tail[0] is not None:
                                    pending_tail[0]()
                                    pending_tail[0] = None
                                if kc == 4 and pending_tail[1] is not None:
                                    pending_tail[1]()
                                    pending_tail[1] = None
                                ewaits = [w_scores]
                                if idx >= 2:
                                    ewaits.append(("s_mm", pv_hist[idx - 2]))
                                sch.on("scalar",
                                       (lambda e, bank=bank, kc=kc: e.activation(
                                           p_sb[kc % 2][:, :], bank[:, :], AF.Exp,
                                           bias=negc[:, :], scale=1.0)),
                                       waits=ewaits, incs=[("s_act", 1)])
                                exp_hist.append(sch.count["s_act"])
                                sch.on("tensor",
                                       (lambda e, kc=kc, pp=pp, hh=hh: e.matmul(
                                           psC[0:65, 0:512],
                                           v_sb[pp][:, kc * 130 + hh * 65: kc * 130 + hh * 65 + 65],
                                           p_sb[kc % 2][:, 0:512],
                                           start=(kc == 0), stop=(kc == NKC - 1))),
                                       waits=[sch.w("s_act")] + ([sch.w("s_dve")] if kc == 0 else []))
                                sch.on("tensor",
                                       (lambda e, kc=kc, pp=pp, hh=hh: e.matmul(
                                           psC[0:65, 512:1024],
                                           v_sb[pp][:, kc * 130 + hh * 65: kc * 130 + hh * 65 + 65],
                                           p_sb[kc % 2][:, 512:1024],
                                           start=(kc == 0), stop=(kc == NKC - 1))),
                                       incs=[("s_mm", 1)])
                                pv_hist.append(sch.count["s_mm"])
                            if head == 2 * pair and qt == 0:
                                head_seed_done[head] = sch.count["s_mm"]
                            pending_tail[0], pending_tail[1] = make_tail(
                                q0, hh, sch.count["s_mm"])
                        if head < H - 2:
                            fetch_g(head + 2, [("s_mm", sch.count["s_mm"])])
                    pair_kv_done[pair] = sch.count["s_mm"]
                    if pending_tail[0] is not None:
                        pending_tail[0]()
                        pending_tail[0] = None
                    if pending_tail[1] is not None:
                        pending_tail[1]()
                        pending_tail[1] = None
                    wctx = [sch.w("s_dve"), sch.w("s_gp")]
                    sch.on("sync",
                           (lambda e, pair=pair: e.dma_start(
                               bass.AP(wo_sb[0], 0, [[1024, 128], [1, 1024]]),
                               wo_in[pair * 128:(pair + 1) * 128, :])),
                           waits=([("s_mm", pair_mm_done[pair - 1])] if pair >= 1 else []),
                           incs=[("s_ld", 16)])
                    wwo = sch.w("s_ld")
                    nreg = 0
                    evac_hist = []
                    for qt in range(2):
                        q0 = qt * 1024
                        for oc in range(NCH):
                            for jh in range(2):
                                reg = nreg % 4
                                tgt = [psD, psB][reg // 2]
                                cols = (reg % 2) * 512
                                waits = wctx + [wwo]
                                if nreg >= 4:
                                    waits.append(("s_dve", evac_hist[nreg - 4]))
                                elif reg // 2 == 1:
                                    waits.append(sch.w("s_act"))
                                sch.on("tensor",
                                       (lambda e, tgt=tgt, cols=cols, pp=pp, oc=oc, q0=q0, jh=jh: e.matmul(
                                           tgt[:, cols:cols + 512],
                                           wo_sb[pp][:, oc * 128:(oc + 1) * 128],
                                           ctx_sb[:, q0 + jh * 512:q0 + (jh + 1) * 512],
                                           start=True, stop=True)),
                                       waits=waits, incs=[("s_mm", 1)])
                                sch.on("vector",
                                       (lambda e, tgt=tgt, cols=cols, oc=oc, q0=q0, jh=jh: e.tensor_tensor(
                                           x1[oc][:, q0 + jh * 512:q0 + (jh + 1) * 512],
                                           x1[oc][:, q0 + jh * 512:q0 + (jh + 1) * 512],
                                           tgt[:, cols:cols + 512], ALU.add)),
                                       waits=[sch.w("s_mm")], incs=[("s_dve", 1)])
                                evac_hist.append(sch.count["s_dve"])
                                nreg += 1
                    pair_mm_done[pair] = sch.count["s_mm"]
                sch.flush()

            sc4 = contextlib.ExitStack()
            tmp0 = sc4.enter_context(nc.sbuf_tensor(f"tmp0b_s{seq}", [128, 512], F32))
            xsq = [sc4.enter_context(nc.sbuf_tensor(f"xsqb_{i}_s{seq}", [128, 512], F32))
                   for i in range(2)]
            for tt in range(4):
                t0 = tt * 512
                for dc in range(NCH):
                    sch.on("tensor",
                           (lambda e, dc=dc, t0=t0: e.matmul(
                               psA[0:1, 0:512], ones[:, 0:1], x1[dc][:, t0:t0 + 512],
                               start=(dc == 0), stop=(dc == NCH - 1))),
                           waits=[sch.w("s_dve")] if dc == 0 else [],
                           incs=([("s_mm", 1)] if dc == NCH - 1 else []))
                for dc in range(NCH):
                    sch.on("scalar",
                           (lambda e, dc=dc, t0=t0: e.activation(
                               xsq[dc % 2][:, :], x1[dc][:, t0:t0 + 512], AF.Square)),
                           waits=[sch.w("s_mm")], incs=[("s_act", 1)])
                    sch.on("tensor",
                           (lambda e, dc=dc, st=(dc == 0): e.matmul(
                               psA[0:1, 512:1024], ones[:, 0:1], xsq[dc % 2][:, :],
                               start=st, stop=(dc == NCH - 1))),
                           waits=[sch.w("s_act")], incs=[("s_mm", 1)])
                wmm = sch.w("s_mm")
                sch.on("vector", lambda e: e.tensor_scalar(
                    rows[0:1, 0:512], psA[0:1, 0:512], 1.0 / D, None, ALU.mult), waits=[wmm])
                sch.on("vector", lambda e: e.tensor_scalar(
                    rows[0:1, 512:1024], psA[0:1, 512:1024], 1.0 / D, None, ALU.mult))
                sch.on("vector", lambda e: e.tensor_tensor(
                    rows[0:1, 1024:1536], rows[0:1, 0:512], rows[0:1, 0:512], ALU.mult))
                sch.on("vector", lambda e: e.tensor_tensor(
                    rows[0:1, 1536:2048], rows[0:1, 512:1024], rows[0:1, 1024:1536], ALU.subtract))
                sch.on("vector", lambda e: e.tensor_scalar(
                    rows[0:1, 512:1024], rows[0:1, 1536:2048], 1e-5, None, ALU.add),
                    incs=[("s_dve", 1)])
                sch.on("scalar", lambda e: e.activation(
                    rows[0:1, 1024:1536], rows[0:1, 512:1024], AF.Sqrt),
                    waits=[sch.w("s_dve")], incs=[("s_act", 1)])
                sch.on("vector", lambda e: e.reciprocal(
                    rows[0:1, 1536:2048], rows[0:1, 1024:1536]),
                    waits=[sch.w("s_act")], incs=[("s_dve", 1)])
                sch.on("tensor", lambda e: e.matmul(
                    psB[:, 0:512], ones[0:1, 0:128], rows[0:1, 0:512],
                    start=True, stop=True), waits=[sch.w("s_dve")])
                sch.on("tensor", lambda e: e.matmul(
                    psB[:, 512:1024], ones[0:1, 0:128], rows[0:1, 1536:2048],
                    start=True, stop=True), incs=[("s_mm", 1)])
                wmm = sch.w("s_mm")
                for dc in range(NCH):
                    sch.on("vector",
                           (lambda e, dc=dc, t0=t0: e.tensor_tensor(
                               tmp0[:, :], x1[dc][:, t0:t0 + 512], psB[:, 0:512], ALU.subtract)),
                           waits=[wmm] if dc == 0 else [])
                    sch.on("vector",
                           (lambda e, dc=dc, t0=t0: e.tensor_tensor(
                               x1[dc][:, t0:t0 + 512], tmp0[:, :], psB[:, 512:1024], ALU.mult)),
                           incs=[("s_dve", 1)])
            sch.flush()
            sc4.close()

            with contextlib.ExitStack() as sc5:
                s5sb = lambda name, shape: sc5.enter_context(nc.sbuf_tensor(f"{name}_s{seq}", shape, F32))
                base_sb = s5sb("base_sb", [128, NCH * 1024])
                x2 = [s5sb(f"x2_{c}", [128, 1024]) for c in range(NCH)]
                gexp = s5sb("gexp", [128, 80])
                gsum = s5sb("gsum", [128, 8])
                gm1 = s5sb("gm1", [128, 8])
                gm2 = s5sb("gm2", [128, 8])
                gm3 = s5sb("gm3", [128, 8])
                grc = s5sb("grc", [128, 8])
                gt1 = s5sb("gt1", [128, 80])
                gt2 = s5sb("gt2", [128, 80])
                gsel = s5sb("gsel", [128, 80])
                gew = s5sb("gew", [128, 64])
                ewt = s5sb("ewt", [8, 1024])
                ewfm = s5sb("ewfm", [128, 1024])
                zw = s5sb("zw", [128, 1024])
                xtm2 = s5sb("xtm2", [128, 1024])
                sqj = s5sb("sqj", [128, 1024])
                lrow = [s5sb(f"lrow{i}", [128, 1]) for i in range(6)]
                otile = [s5sb(f"otile{i}", [128, 1024]) for i in range(2)]

                sch.on("sync",
                       (lambda e: e.dma_start(
                           bass.AP(base_sb, 0, [[NCH * 1024, 128], [1024, NCH], [1, 1024]]),
                           bass.AP(bw_in.tensor, 0, [[D, 128], [128 * D, NCH], [1, 1024]]))),
                       incs=[("s_ld", 16)])
                wbase = sch.w("s_ld")

                for tt in range(2):
                    t0 = tt * 1024
                    for tc in range(8):
                        for dc in range(NCH):
                            sch.on("tensor",
                                   (lambda e, tc=tc, dc=dc, t0=t0: e.matmul(
                                       psA[:, tc * 16:tc * 16 + 10],
                                       x1[dc][:, t0 + tc * 128:t0 + (tc + 1) * 128],
                                       gate_sb[:, dc * 10:(dc + 1) * 10],
                                       start=(dc == 0), stop=(dc == NCH - 1))),
                                   waits=([sch.w("s_dve")] if (tc == 0 and dc == 0) else []),
                                   incs=([("s_mm", 1)] if dc == NCH - 1 else []))
                    wg = sch.w("s_mm")
                    for tc in range(8):
                        sch.on("scalar",
                               (lambda e, tc=tc: e.activation(
                                   gexp[:, tc * 10:(tc + 1) * 10], psA[:, tc * 16:tc * 16 + 10],
                                   AF.Exp, accum_out=gsum[:, tc:tc + 1])),
                               waits=[wg], incs=[("s_act", 1)])
                    wexp = sch.w("s_act")
                    for tc in range(8):
                        sch.on("vector",
                               (lambda e, tc=tc: e.reciprocal(
                                   grc[:, tc:tc + 1], gsum[:, tc:tc + 1])),
                               waits=[wexp] if tc == 0 else [],
                               incs=([("s_dve", 1)] if tc == 7 else []))
                    wb = sch.w("s_dve")
                    for tc in range(8):
                        sch.on("vector",
                               (lambda e, tc=tc: e.tensor_tensor(
                                   gt1[:, tc * 10:(tc + 1) * 10], gexp[:, tc * 10:(tc + 1) * 10],
                                   grc[:, tc:tc + 1].broadcast_to([128, 10]), ALU.mult)),
                               waits=[wb] if tc == 0 else [])
                    for tc in range(8):
                        sch.on("vector",
                               (lambda e, tc=tc: e.reduce_max(
                                   gm1[:, tc:tc + 1], gt1[:, tc * 10:(tc + 1) * 10], axis=AX.X)),
                               incs=([("s_dve", 1)] if tc == 7 else []))
                    wb = sch.w("s_dve")
                    for tc in range(8):
                        sch.on("vector",
                               (lambda e, tc=tc: e.tensor_tensor(
                                   gsel[:, tc * 10:(tc + 1) * 10], gt1[:, tc * 10:(tc + 1) * 10],
                                   gm1[:, tc:tc + 1].broadcast_to([128, 10]), ALU.is_equal)),
                               waits=[wb] if tc == 0 else [],
                               incs=([("s_dve", 1)] if tc == 7 else []))
                    sch.on("vector", lambda e: e.tensor_scalar(
                        gt2[:, :], gsel[:, :], -1e9, None, ALU.mult),
                        waits=[sch.w("s_dve")], incs=[("s_dve", 1)])
                    sch.on("vector", lambda e: e.tensor_tensor(
                        gt2[:, :], gt1[:, :], gt2[:, :], ALU.add),
                        waits=[sch.w("s_dve")], incs=[("s_dve", 1)])
                    for tc in range(8):
                        sch.on("vector",
                               (lambda e, tc=tc: e.reduce_max(
                                   gm2[:, tc:tc + 1], gt2[:, tc * 10:(tc + 1) * 10], axis=AX.X)),
                               waits=[sch.w("s_dve")] if tc == 0 else [],
                               incs=([("s_dve", 1)] if tc == 7 else []))
                    wb = sch.w("s_dve")
                    for tc in range(8):
                        sch.on("vector",
                               (lambda e, tc=tc: e.tensor_tensor(
                                   gt2[:, tc * 10:(tc + 1) * 10], gt2[:, tc * 10:(tc + 1) * 10],
                                   gm2[:, tc:tc + 1].broadcast_to([128, 10]), ALU.is_equal)),
                               waits=[wb] if tc == 0 else [],
                               incs=([("s_dve", 1)] if tc == 7 else []))
                    sch.on("vector", lambda e: e.tensor_tensor(
                        gsel[:, :], gsel[:, :], gt2[:, :], ALU.add),
                        waits=[sch.w("s_dve")], incs=[("s_dve", 1)])
                    for tc in range(8):
                        sch.on("vector",
                               (lambda e, tc=tc: e.tensor_tensor(
                                   gew[:, tc * 8:(tc + 1) * 8], gt1[:, tc * 10:tc * 10 + 8],
                                   gsel[:, tc * 10:tc * 10 + 8], ALU.mult)),
                               waits=[sch.w("s_dve")] if tc == 0 else [],
                               incs=([("s_dve", 1)] if tc == 7 else []))
                    for tc in range(8):
                        sch.on("vector",
                               (lambda e, tc=tc: e.reduce_sum(
                                   gm2[:, tc:tc + 1], gew[:, tc * 8:(tc + 1) * 8], axis=AX.X)),
                               waits=[sch.w("s_dve")] if tc == 0 else [],
                               incs=([("s_dve", 1)] if tc == 7 else []))
                    sch.on("vector", lambda e: e.tensor_scalar(
                        gm3[:, :], gm2[:, :], 0.0, None, ALU.is_equal),
                        waits=[sch.w("s_dve")], incs=[("s_dve", 1)])
                    sch.on("vector", lambda e: e.tensor_tensor(
                        gm2[:, :], gm2[:, :], gm3[:, :], ALU.add),
                        waits=[sch.w("s_dve")], incs=[("s_dve", 1)])
                    wdz = sch.w("s_dve")
                    for tc in range(8):
                        sch.on("vector",
                               (lambda e, tc=tc: e.reciprocal(
                                   grc[:, tc:tc + 1], gm2[:, tc:tc + 1])),
                               waits=[wdz] if tc == 0 else [],
                               incs=([("s_dve", 1)] if tc == 7 else []))
                    sch.on("vector", lambda e: e.tensor_scalar(
                        grc[:, :], grc[:, :], 2.0, None, ALU.mult),
                        waits=[sch.w("s_dve")], incs=[("s_dve", 1)])
                    wb = sch.w("s_dve")
                    for tc in range(8):
                        sch.on("vector",
                               (lambda e, tc=tc: e.tensor_tensor(
                                   gew[:, tc * 8:(tc + 1) * 8], gew[:, tc * 8:(tc + 1) * 8],
                                   grc[:, tc:tc + 1].broadcast_to([128, 8]), ALU.mult)),
                               waits=[wb] if tc == 0 else [],
                               incs=([("s_dve", 1)] if tc == 7 else []))
                    for tc in range(8):
                        sch.on("tensor",
                               (lambda e, tc=tc: e.transpose(
                                   psB[0:8, tc * 128:(tc + 1) * 128],
                                   gew[:, tc * 8:(tc + 1) * 8], ident[:, :])),
                               waits=[sch.w("s_dve")] if tc == 0 else [],
                               incs=[("s_mm", 1)])
                        sch.on("vector",
                               (lambda e, tc=tc: e.tensor_copy(
                                   ewt[0:8, tc * 128:(tc + 1) * 128],
                                   psB[0:8, tc * 128:(tc + 1) * 128])),
                               waits=[sch.w("s_mm")], incs=[("s_dve", 1)])
                        sch.on("tensor",
                               (lambda e, tc=tc: e.matmul(
                                   psC[:, tc * 128:(tc + 1) * 128], rbc_sb[:, :],
                                   ewt[0:8, tc * 128:(tc + 1) * 128], start=True, stop=True)),
                               waits=[sch.w("s_dve")], incs=[("s_mm", 1)])
                    sch.on("vector", lambda e: e.tensor_copy(ewfm[:, :], psC[:, :]),
                           waits=[sch.w("s_mm")], incs=[("s_dve", 1)])
                    for jh in range(2):
                        for dc in range(NCH):
                            sch.on("tensor",
                                   (lambda e, jh=jh, dc=dc, t0=t0: e.matmul(
                                       psD[:, jh * 512:(jh + 1) * 512],
                                       la_sb[:, dc * 128:(dc + 1) * 128],
                                       x1[dc][:, t0 + jh * 512:t0 + jh * 512 + 512],
                                       start=(dc == 0), stop=(dc == NCH - 1))),
                                   waits=([sch.w("s_dve")] if (jh == 0 and dc == 0) else []),
                                   incs=([("s_mm", 1)] if dc == NCH - 1 else []))
                    sch.on("vector", lambda e: e.tensor_tensor(
                        zw[:, :], psD[:, :], ewfm[:, :], ALU.mult),
                        waits=[sch.w("s_mm")], incs=[("s_dve", 1)])
                    wzw = sch.w("s_dve")
                    evh = []
                    for oc in range(NCH):
                        tgt = [psA, psB][oc % 2]
                        waits = [wbase, wzw]
                        if oc >= 2:
                            waits.append(("s_dve", evh[oc - 2]))
                        for jh in range(2):
                            for dc in range(NCH):
                                sch.on("tensor",
                                       (lambda e, oc=oc, jh=jh, dc=dc, t0=t0, tgt=tgt: e.matmul(
                                           tgt[:, jh * 512:(jh + 1) * 512],
                                           base_sb[:, dc * 1024 + oc * 128:dc * 1024 + (oc + 1) * 128],
                                           x1[dc][:, t0 + jh * 512:t0 + jh * 512 + 512],
                                           start=(dc == 0), stop=False)),
                                       waits=(waits if (jh == 0 and dc == 0) else []))
                            sch.on("tensor",
                                   (lambda e, oc=oc, jh=jh, tgt=tgt: e.matmul(
                                       tgt[:, jh * 512:(jh + 1) * 512],
                                       lb_sb[:, oc * 128:(oc + 1) * 128],
                                       zw[:, jh * 512:(jh + 1) * 512],
                                       start=False, stop=True)),
                                   incs=[("s_mm", 1)])
                        sch.on("vector",
                               (lambda e, oc=oc, tgt=tgt: e.tensor_scalar(
                                   x2[oc][:, :], tgt[:, :], bbc[:, oc:oc + 1], None, ALU.add)),
                               waits=[sch.w("s_mm")])
                        sch.on("vector",
                               (lambda e, oc=oc, t0=t0: e.tensor_tensor(
                                   x2[oc][:, :], x2[oc][:, :], x1[oc][:, t0:t0 + 1024], ALU.add)),
                               incs=[("s_dve", 1)])
                        evh.append(sch.count["s_dve"])
                    wffn = sch.w("s_dve")
                    for tc in range(8):
                        tgt = [psD, psC][tc % 2]
                        for dc in range(NCH):
                            sch.on("tensor",
                                   (lambda e, tc=tc, dc=dc, tgt=tgt: e.transpose(
                                       tgt[:, dc * 128:(dc + 1) * 128],
                                       x2[dc][:, tc * 128:(tc + 1) * 128], ident[:, :])),
                                   waits=([wffn, sch.w("s_dve")] if dc == 0 else []),
                                   incs=([("s_mm", 1)] if dc == NCH - 1 else []))
                        sch.on("vector",
                               (lambda e, tgt=tgt: e.tensor_copy(xtm2[:, :], tgt[:, :])),
                               waits=[sch.w("s_mm")])
                        sch.on("vector", lambda e: e.reduce_sum(
                            lrow[0][:, :], xtm2[:, :], axis=AX.X), incs=[("s_dve", 1)])
                        sch.on("scalar", lambda e: e.activation(
                            sqj[:, :], xtm2[:, :], AF.Square, accum_out=lrow[1][:, :]),
                            waits=[sch.w("s_dve")], incs=[("s_act", 1)])
                        sch.on("vector", lambda e: e.tensor_scalar(
                            lrow[2][:, :], lrow[0][:, :], 1.0 / D, None, ALU.mult),
                            waits=[sch.w("s_act")])
                        sch.on("vector", lambda e: e.tensor_scalar(
                            lrow[3][:, :], lrow[1][:, :], 1.0 / D, None, ALU.mult),
                            incs=[("s_dve", 1)])
                        sch.on("vector", lambda e: e.tensor_tensor(
                            lrow[4][:, :], lrow[2][:, :], lrow[2][:, :], ALU.mult),
                            waits=[sch.w("s_dve")], incs=[("s_dve", 1)])
                        sch.on("vector", lambda e: e.tensor_tensor(
                            lrow[4][:, :], lrow[3][:, :], lrow[4][:, :], ALU.subtract),
                            waits=[sch.w("s_dve")], incs=[("s_dve", 1)])
                        sch.on("vector", lambda e: e.tensor_scalar(
                            lrow[4][:, :], lrow[4][:, :], 1e-5, None, ALU.add),
                            waits=[sch.w("s_dve")], incs=[("s_dve", 1)])
                        sch.on("scalar", lambda e: e.activation(
                            lrow[5][:, :], lrow[4][:, :], AF.Sqrt),
                            waits=[sch.w("s_dve")], incs=[("s_act", 1)])
                        sch.on("vector", lambda e: e.reciprocal(lrow[4][:, :], lrow[5][:, :]),
                               waits=[sch.w("s_act")], incs=[("s_dve", 1)])
                        ob = otile[tc % 2]
                        sch.on("vector",
                               (lambda e, ob=ob: e.tensor_tensor(
                                   ob[:, :], xtm2[:, :],
                                   lrow[2][:, :].broadcast_to([128, 1024]), ALU.subtract)),
                               waits=[("s_out", max(sch.flush_base["s_out"],
                                                    sch.count["s_out"] - 16)),
                                      sch.w("s_dve")])
                        sch.on("vector",
                               (lambda e, ob=ob: e.tensor_tensor(
                                   ob[:, :], ob[:, :],
                                   lrow[4][:, :].broadcast_to([128, 1024]), ALU.mult)),
                               incs=[("s_dve", 1)])
                        sch.on("sync",
                               (lambda e, ob=ob, tc=tc, t0=t0: e.dma_start(
                                   y_out[seq, t0 + tc * 128:t0 + (tc + 1) * 128, :], ob[:, :])),
                               waits=[sch.w("s_dve")], incs=[("s_out", 16)])
                sch.flush()

        sch.on("gpsimd", lambda e: e.nop(), waits=[sch.w("s_out")])
        sch.flush()

    return nc


_NC_CACHE = None


def kernel(**inputs):
    global _NC_CACHE
    if _NC_CACHE is None:
        _NC_CACHE = build()
    nc = _NC_CACHE

    x = np.ascontiguousarray(inputs["L0_attn_hidden_states"], dtype=np.float32)
    base = {
        "wq": np.ascontiguousarray(inputs["L0_attn_Wq"], np.float32),
        "wk": np.ascontiguousarray(inputs["L0_attn_Wk"], np.float32),
        "wv": np.ascontiguousarray(inputs["L0_attn_Wv"], np.float32),
        "wo": np.ascontiguousarray(inputs["L0_attn_Wo"], np.float32),
        "ln_g": np.ascontiguousarray(inputs["L0_attn_ln_g"], np.float32),
        "ln_b": np.ascontiguousarray(inputs["L0_attn_ln_b"], np.float32),
        "rel_bias": np.ascontiguousarray(inputs["L0_attn_rel_bias"], np.float32),
        "base_wt": np.ascontiguousarray(np.asarray(inputs["L0_ffn_base_w"], np.float32).T),
        "base_b": np.ascontiguousarray(inputs["L0_ffn_base_b"], np.float32),
        "gate_wt": np.ascontiguousarray(np.asarray(inputs["L0_ffn_gate_w"], np.float32).T),
        "lora_at": np.ascontiguousarray(
            np.asarray(inputs["L0_ffn_lora_A"], np.float32).transpose(2, 0, 1).reshape(D, 128)),
        "lora_bt": np.ascontiguousarray(
            np.asarray(inputs["L0_ffn_lora_B"], np.float32).transpose(0, 2, 1).reshape(128, D)),
        "onehot": _bucket_onehot(),
        "ident": np.eye(128, dtype=np.float32),
        "rbc": _rbc_const(),
    }
    in_maps = [dict(base, x=np.ascontiguousarray(x[c * BPC:(c + 1) * BPC]))
               for c in range(NCORES)]
    res = run_bass_kernel_spmd(nc, in_maps, core_ids=list(range(NCORES)))
    return np.concatenate([r["y"] for r in res.results], axis=0)


# version 16 (final)
# speedup vs baseline: 1.0952x; 1.0952x over previous
import contextlib
import math
import numpy as np

import concourse.bass as bass
import concourse.mybir as mybir
from concourse.bass_utils import run_bass_kernel_spmd

F32 = mybir.dt.float32
F32R = mybir.dt.float32r
BF16 = mybir.dt.bfloat16
AF = mybir.ActivationFunctionType
ALU = mybir.AluOpType
AX = mybir.AxisListType

NCORES = 8
B = 16
BPC = B // NCORES
S = 2048
D = 1024
H = 16
DK = 64
NCH = D // 128
NKC = S // 128
GW = 3968
NEGC = -12.0



def _bucket_onehot():
    rel = np.arange(-2047, 2048)
    ret = (rel > 0).astype(np.int32) * 16
    n = np.abs(rel)
    n_safe = np.maximum(n, 1).astype(np.float32)
    val = 8 + (np.log(n_safe / np.float32(8.0)) / np.log(16.0) * 8).astype(np.int32)
    val = np.minimum(val, 15)
    bk = ret + np.where(n < 8, n, val)
    oh = np.zeros((32, 4096), np.float32)
    oh[bk, np.arange(4095)] = 1.0
    return oh


def _rbc_const():
    r = np.zeros((8, 128), np.float32)
    for e in range(8):
        r[e, e * 16:(e + 1) * 16] = 1.0
    return r



DMA_SEMS = ("s_ld", "s_gp", "s_out")


class Sched:
    def __init__(self, nc, sems):
        self.nc = nc
        self.sems = sems
        self.count = {k: 0 for k in sems}
        self.items = []
        self.flush_base = {k: 0 for k in sems}
        self.sv_count = 0

    def w(self, s):
        return (s, self.count[s])

    def on(self, eng, fn, waits=(), incs=()):
        self.items.append((eng, list(waits), fn, list(incs)))
        for s, n in incs:
            self.count[s] += n

    def flush(self):
        waited = {s: set() for s in DMA_SEMS}
        for _, waits, _, _ in self.items:
            for s, v in waits:
                if s in waited and v > 0:
                    waited[s].add(v)
        for s in DMA_SEMS:
            if self.count[s] > self.flush_base[s]:
                waited[s].add(self.count[s])
        per = {}
        for it in self.items:
            per.setdefault(it[0], []).append(it)
        self.items = []
        run = dict(self.flush_base)
        with self.nc.Block() as block:
            for engname, ops in per.items():
                def body(e, ops=ops, engname=engname):
                    last = {}
                    cert = None
                    for _, waits, fn, incs in ops:
                        for s, v in waits:
                            if v <= 0 or last.get(s, -1) >= v:
                                continue
                            e.wait_ge(self.sems[s], v)
                            last[s] = v
                        if engname == "vector" and cert is not None:
                            e.wait_ge(self.sems[cert[0]], cert[1])
                            last[cert[0]] = max(last.get(cert[0], 0), cert[1])
                        inst = fn(e)
                        if engname == "vector" and not incs:
                            self.sv_count += 1
                            inst = inst.then_inc(self.sems["s_sv"], 1)
                            cert = ("s_sv", self.sv_count)
                        for s, n in incs:
                            inst = inst.then_inc(self.sems[s], n)
                            run[s] += n
                            if engname == "vector":
                                cert = (s, run[s])
                        for s, n in incs:
                            if s in waited and run[s] in waited[s]:
                                e.wait_ge(self.sems[s], run[s])
                                last[s] = run[s]
                getattr(block, engname)(body)
        self.flush_base = dict(self.count)



def build():
    nc = bass.Bass("TRN2", target_bir_lowering=False)

    ap = lambda *a, **k: nc.dram_tensor(*a, **k).ap()
    x_in = ap("x", [BPC, S, D], F32, kind="ExternalInput")
    wq_in = ap("wq", [D, D], F32, kind="ExternalInput")
    wk_in = ap("wk", [D, D], F32, kind="ExternalInput")
    wv_in = ap("wv", [D, D], F32, kind="ExternalInput")
    wo_in = ap("wo", [D, D], F32, kind="ExternalInput")
    lng_in = ap("ln_g", [D], F32, kind="ExternalInput")
    lnb_in = ap("ln_b", [D], F32, kind="ExternalInput")
    rel_in = ap("rel_bias", [32, H], F32, kind="ExternalInput")
    bw_in = ap("base_wt", [D, D], F32, kind="ExternalInput")
    bb_in = ap("base_b", [D], F32, kind="ExternalInput")
    gw_in = ap("gate_wt", [D, 10], F32, kind="ExternalInput")
    la_in = ap("lora_at", [D, 128], F32, kind="ExternalInput")
    lb_in = ap("lora_bt", [128, D], F32, kind="ExternalInput")
    oh_in = ap("onehot", [32, 4096], F32, kind="ExternalInput")
    id_in = ap("ident", [128, 128], F32, kind="ExternalInput")
    rbc_in = ap("rbc", [8, 128], F32, kind="ExternalInput")
    y_out = ap("y", [BPC, S, D], F32, kind="ExternalOutput")

    rbd_hbm = ap("rbd_scr", [H, 4096], F32, kind="Internal")
    k_scr = ap("k_scr", [BPC, H, 128, S], BF16, kind="Internal")
    q_scr = ap("q_scr", [BPC, H, 128, S], BF16, kind="Internal")
    v_scr = ap("v_scr", [BPC, H, S, DK], F32, kind="Internal")

    with contextlib.ExitStack() as root:
        sb = lambda name, shape, dt=F32: root.enter_context(nc.sbuf_tensor(name, shape, dt))
        sems = {n: root.enter_context(nc.semaphore(n))
                for n in ["s_ld", "s_gp", "s_mm", "s_act", "s_dve", "s_out", "s_sv"]}
        sch = Sched(nc, sems)

        x1 = [sb(f"x1_{c}", [128, S]) for c in range(NCH)]
        ones = sb("ones", [128, 128])
        ident = sb("ident_sb", [128, 128])
        ident_r = sb("ident_r", [128, 128], F32R)
        negc = sb("negc", [128, 1])
        lng = sb("lng", [128, NCH])
        lnb = sb("lnb", [128, NCH])
        bbc = sb("bbc", [128, NCH])
        gate_sb = sb("gate_sb", [128, NCH * 10])
        la_sb = sb("la_sb", [128, D])
        lb_sb = sb("lb_sb", [128, D])
        rbc_sb = sb("rbc_sb", [8, 128])
        rows = sb("rows", [1, 1536])

        psA = root.enter_context(nc.psum_tensor("psA", [128, 1024], F32))
        psB = root.enter_context(nc.psum_tensor("psB", [128, 1024], F32))
        psC = root.enter_context(nc.psum_tensor("psC", [128, 1024], F32))
        psD = root.enter_context(nc.psum_tensor("psD", [128, 1024], F32))

        with contextlib.ExitStack() as pre:
            psb = lambda name, shape, dt=F32: pre.enter_context(nc.sbuf_tensor(name, shape, dt))
            oh_sb = psb("oh_sb", [32, 4096])
            rel_sb = psb("rel_sb", [32, H])
            rbdT = psb("rbdT", [16, 4096])

            def L(dst, src):
                return lambda e, d=dst, s=src: e.dma_start(d, s)
            sch.on("sync", L(ident[:, :], id_in), incs=[("s_ld", 16)])
            sch.on("sync", L(oh_sb[:, :], oh_in), incs=[("s_ld", 16)])
            sch.on("sync", L(rel_sb[:, :], rel_in), incs=[("s_ld", 16)])
            sch.on("sync", L(rbc_sb[:, :], rbc_in), incs=[("s_ld", 16)])
            for dst, src in [(lng, lng_in), (lnb, lnb_in), (bbc, bb_in)]:
                sch.on("sync",
                       (lambda e, d=dst, s=src:
                        e.dma_start(bass.AP(d, 0, [[NCH, 128], [1, NCH]]),
                                    bass.AP(s.tensor, 0, [[1, 128], [128, NCH]]),
                                    allow_slow_non_contiguous=True)),
                       incs=[("s_ld", 16)])
            sch.on("sync",
                   (lambda e: e.dma_start(
                       bass.AP(gate_sb, 0, [[NCH * 10, 128], [10, NCH], [1, 10]]),
                       bass.AP(gw_in.tensor, 0, [[10, 128], [1280, NCH], [1, 10]]))),
                   incs=[("s_ld", 16)])
            sch.on("sync",
                   (lambda e: e.dma_start(
                       bass.AP(la_sb, 0, [[D, 128], [128, NCH], [1, 128]]),
                       bass.AP(la_in.tensor, 0, [[128, 128], [16384, NCH], [1, 128]]))),
                   incs=[("s_ld", 16)])
            sch.on("sync", L(lb_sb[:, :], lb_in), incs=[("s_ld", 16)])
            sch.on("gpsimd", L(ident_r[:, :], id_in), incs=[("s_gp", 16)])
            sch.on("vector", lambda e: e.memset(ones[:, :], 1.0), incs=[("s_dve", 1)])
            sch.on("vector", lambda e: e.memset(negc[:, :], NEGC), incs=[("s_dve", 1)])
            wld = sch.w("s_ld")
            for g in range(8):
                half = g % 2
                sch.on("tensor",
                       (lambda e, g=g, half=half: e.matmul(
                           psA[0:16, half * 512:(half + 1) * 512],
                           rel_sb[:, :], oh_sb[:, g * 512:(g + 1) * 512],
                           start=True, stop=True)),
                       waits=[wld, sch.w("s_dve")] if g == 0 else [sch.w("s_dve")],
                       incs=[("s_mm", 1)])
                sch.on("vector",
                       (lambda e, g=g, half=half: e.tensor_copy(
                           rbdT[:, g * 512:(g + 1) * 512],
                           psA[0:16, half * 512:(half + 1) * 512])),
                       waits=[sch.w("s_mm")], incs=[("s_dve", 1)])
            sch.on("gpsimd", lambda e: e.dma_start(rbd_hbm, rbdT[:, :]),
                   waits=[sch.w("s_dve")], incs=[("s_gp", 16)])
            sch.flush()

        for seq in range(BPC):
            with contextlib.ExitStack() as sc1:
                s1sb = lambda name, shape: sc1.enter_context(nc.sbuf_tensor(f"{name}_s{seq}", shape, F32))
                normed_h = [sc1.enter_context(nc.sbuf_tensor(f"nrmh_{c}_s{seq}", [128, S], BF16))
                            for c in range(NCH)]
                normed_l = [sc1.enter_context(nc.sbuf_tensor(f"nrml_{c}_s{seq}", [128, S], BF16))
                            for c in range(NCH)]
                x_tm = [s1sb(f"xtm_{i}", [128, 1024]) for i in range(2)]
                tmp0 = s1sb("tmp0", [128, 512])
                xsq = [s1sb(f"xsq_{i}", [128, 512]) for i in range(2)]

                for tt in range(4):
                    t0 = tt * 512
                    tr_done = []
                    for tc in range(4):
                        waits = [("s_mm", tr_done[tc - 2])] if tc >= 2 else [sch.w("s_dve")]
                        sch.on("sync",
                               (lambda e, tc=tc, t0=t0: e.dma_start(
                                   x_tm[tc % 2][:, :],
                                   x_in[seq, t0 + tc * 128:t0 + (tc + 1) * 128, :])),
                               waits=waits, incs=[("s_ld", 16)])
                        wld = sch.w("s_ld")
                        for dc in range(NCH):
                            pst = [psA, psB, psC, psD][dc // 2]
                            half = dc % 2
                            sch.on("tensor",
                                   (lambda e, dc=dc, tc=tc, half=half, pst=pst: e.transpose(
                                       pst[:, half * 512 + tc * 128: half * 512 + (tc + 1) * 128],
                                       x_tm[tc % 2][:, dc * 128:(dc + 1) * 128], ident[:, :])),
                                   waits=([wld, sch.w("s_dve")] if dc == 0 else []),
                                   incs=([("s_mm", 1)] if dc == NCH - 1 else []))
                        tr_done.append(sch.count["s_mm"])
                    for dc in range(NCH):
                        pst = [psA, psB, psC, psD][dc // 2]
                        half = dc % 2
                        sch.on("vector",
                               (lambda e, dc=dc, half=half, t0=t0, pst=pst: e.tensor_copy(
                                   x1[dc][:, t0:t0 + 512], pst[:, half * 512:(half + 1) * 512])),
                               waits=[("s_mm", tr_done[3])], incs=[("s_dve", 1)])
                    for dc in range(NCH):
                        sch.on("tensor",
                               (lambda e, dc=dc, st=(dc == 0), t0=t0: e.matmul(
                                   psA[0:1, 0:512], ones[:, 0:1], x1[dc][:, t0:t0 + 512],
                                   start=st, stop=(dc == NCH - 1))),
                               waits=[sch.w("s_dve")],
                               incs=([("s_mm", 1)] if dc == NCH - 1 else []))
                    for dc in range(NCH):
                        sch.on("scalar",
                               (lambda e, dc=dc, t0=t0: e.activation(
                                   xsq[dc % 2][:, :], x1[dc][:, t0:t0 + 512], AF.Square)),
                               waits=[sch.w("s_dve"), sch.w("s_mm")],
                               incs=[("s_act", 1)])
                        sch.on("tensor",
                               (lambda e, dc=dc, st=(dc == 0): e.matmul(
                                   psA[0:1, 512:1024], ones[:, 0:1], xsq[dc % 2][:, :],
                                   start=st, stop=(dc == NCH - 1))),
                               waits=[sch.w("s_act")], incs=[("s_mm", 1)])
                    wmm = sch.w("s_mm")
                    sch.on("vector", lambda e: e.tensor_scalar(
                        rows[0:1, 0:512], psA[0:1, 0:512], 1.0 / D, None, ALU.mult),
                        waits=[wmm])
                    sch.on("vector", lambda e: e.tensor_scalar(
                        rows[0:1, 512:1024], psA[0:1, 512:1024], 1.0 / D, None, ALU.mult))
                    sch.on("vector", lambda e: e.tensor_tensor(
                        rows[0:1, 1024:1536], rows[0:1, 0:512], rows[0:1, 0:512], ALU.mult))
                    sch.on("vector", lambda e: e.tensor_tensor(
                        rows[0:1, 512:1024], rows[0:1, 512:1024], rows[0:1, 1024:1536], ALU.subtract))
                    sch.on("vector", lambda e: e.tensor_scalar(
                        rows[0:1, 512:1024], rows[0:1, 512:1024], 1e-6, None, ALU.add),
                        incs=[("s_dve", 1)])
                    sch.on("scalar", lambda e: e.activation(
                        rows[0:1, 1024:1536], rows[0:1, 512:1024], AF.Sqrt),
                        waits=[sch.w("s_dve")], incs=[("s_act", 1)])
                    sch.on("vector", lambda e: e.reciprocal(
                        rows[0:1, 512:1024], rows[0:1, 1024:1536]),
                        waits=[sch.w("s_act")], incs=[("s_dve", 1)])
                    sch.on("tensor", lambda e: e.matmul(
                        psB[:, 0:512], ones[0:1, 0:128], rows[0:1, 0:512],
                        start=True, stop=True), waits=[sch.w("s_dve")])
                    sch.on("tensor", lambda e: e.matmul(
                        psB[:, 512:1024], ones[0:1, 0:128], rows[0:1, 512:1024],
                        start=True, stop=True), incs=[("s_mm", 1)])
                    wmm = sch.w("s_mm")
                    for dc in range(NCH):
                        sch.on("vector",
                               (lambda e, dc=dc, t0=t0: e.tensor_tensor(
                                   tmp0[:, :], x1[dc][:, t0:t0 + 512], psB[:, 0:512], ALU.subtract)),
                               waits=[wmm] if dc == 0 else [])
                        sch.on("vector",
                               (lambda e: e.tensor_tensor(
                                   tmp0[:, :], tmp0[:, :], psB[:, 512:1024], ALU.mult)))
                        sch.on("vector",
                               (lambda e, dc=dc: e.tensor_scalar(
                                   tmp0[:, :], tmp0[:, :],
                                   lng[:, dc:dc + 1], lnb[:, dc:dc + 1], ALU.mult, ALU.add)))
                        sch.on("vector",
                               (lambda e, dc=dc, t0=t0: e.tensor_copy(
                                   normed_h[dc][:, t0:t0 + 512], tmp0[:, :])))
                        sch.on("vector",
                               (lambda e, dc=dc, t0=t0: e.tensor_tensor(
                                   normed_l[dc][:, t0:t0 + 512], tmp0[:, :],
                                   normed_h[dc][:, t0:t0 + 512], ALU.subtract)))
                        sch.on("vector",
                               (lambda e, dc=dc, t0=t0: e.tensor_scalar(
                                   x1[dc][:, t0:t0 + 512], x1[dc][:, t0:t0 + 512],
                                   2.0, None, ALU.mult)),
                               incs=[("s_dve", 1)])
                sch.flush()

                with contextlib.ExitStack() as sc2:
                    s2F = lambda name, shape: sc2.enter_context(nc.sbuf_tensor(f"{name}_s{seq}", shape, F32))
                    s2B = lambda name, shape: sc2.enter_context(nc.sbuf_tensor(f"{name}_s{seq}", shape, BF16))
                    wstg = [s2F(f"wstg{i}", [128, 1024]) for i in range(3)]
                    wqh = [s2B(f"wqh{i}", [128, 1024]) for i in range(2)]
                    wql = [s2B(f"wql{i}", [128, 1024]) for i in range(2)]
                    wkh = [s2B(f"wkh{i}", [128, 1024]) for i in range(2)]
                    wkl = [s2B(f"wkl{i}", [128, 1024]) for i in range(2)]
                    wvh = [s2B(f"wvh{i}", [128, 1024]) for i in range(2)]
                    wvl = [s2B(f"wvl{i}", [128, 1024]) for i in range(2)]
                    ksth = s2B("ksth", [128, 1024])
                    kstl = s2B("kstl", [128, 1024])
                    qsth = s2B("qsth", [128, 1024])
                    qstl = s2B("qstl", [128, 1024])
                    vst = s2F("vst", [128, 1024])

                    mm_done_pair = []
                    stg_free = [0, 0, 0]
                    for pair in range(8):
                        pp = pair % 2
                        c0 = pair * 128
                        for wi, (hi_t, lo_t, srcw) in enumerate(
                                [(wqh[pp], wql[pp], wq_in), (wkh[pp], wkl[pp], wk_in),
                                 (wvh[pp], wvl[pp], wv_in)]):
                            stg = wstg[wi]
                            dwaits = [("s_dve", stg_free[wi])]
                            sch.on("sync",
                                   (lambda e, stg=stg, srcw=srcw, c0=c0: e.dma_start(
                                       bass.AP(stg, 0, [[1024, 128], [128, NCH], [1, 128]]),
                                       bass.AP(srcw.tensor, c0, [[D, 128], [128 * D, NCH], [1, 128]]))),
                                   waits=dwaits, incs=[("s_ld", 16)])
                            wld = sch.w("s_ld")
                            hwaits = [wld]
                            if pair >= 2:
                                hwaits.append(("s_mm", mm_done_pair[pair - 2]))
                            sch.on("vector",
                                   (lambda e, hi_t=hi_t, stg=stg: e.tensor_copy(hi_t[:, :], stg[:, :])),
                                   waits=hwaits)
                            sch.on("vector",
                                   (lambda e, hi_t=hi_t, lo_t=lo_t, stg=stg: e.tensor_tensor(
                                       lo_t[:, :], stg[:, :], hi_t[:, :], ALU.subtract)),
                                   incs=[("s_dve", 1)])
                            stg_free[wi] = sch.count["s_dve"]
                        wsp = sch.w("s_dve")
                        for tt in range(4):
                            t0 = tt * 512
                            half = tt % 2
                            for ph in range(3):
                                wt_t = [wkh[pp], wkh[pp], wkl[pp]][ph]
                                nr = [normed_h, normed_l, normed_h][ph]
                                for dc in range(NCH):
                                    sch.on("tensor",
                                           (lambda e, dc=dc, half=half, wt_t=wt_t, nr=nr, t0=t0,
                                                   st=(ph == 0 and dc == 0),
                                                   sp=(ph == 2 and dc == NCH - 1): e.matmul(
                                               psC[:, half * 512:(half + 1) * 512],
                                               wt_t[:, dc * 128:(dc + 1) * 128],
                                               nr[dc][:, t0:t0 + 512],
                                               start=st, stop=sp)),
                                           waits=([wsp, sch.w("s_dve")] if (ph == 0 and dc == 0) else []),
                                           incs=([("s_mm", 1)] if (ph == 2 and dc == NCH - 1) else []))
                            sch.on("vector",
                                   (lambda e, half=half: e.tensor_copy(
                                       ksth[:, half * 512:(half + 1) * 512],
                                       psC[:, half * 512:(half + 1) * 512])),
                                   waits=[sch.w("s_mm"), sch.w("s_gp")])
                            sch.on("vector",
                                   (lambda e, half=half: e.tensor_tensor(
                                       kstl[:, half * 512:(half + 1) * 512],
                                       psC[:, half * 512:(half + 1) * 512],
                                       ksth[:, half * 512:(half + 1) * 512], ALU.subtract)),
                                   incs=[("s_dve", 1)])
                            for hh in range(2):
                                sch.on("gpsimd",
                                       (lambda e, hh=hh, half=half, pair=pair, t0=t0: e.dma_start(
                                           k_scr[seq, 2 * pair + hh, 0:64, t0:t0 + 512],
                                           ksth[hh * 64:(hh + 1) * 64, half * 512:(half + 1) * 512])),
                                       waits=[sch.w("s_dve")], incs=[("s_gp", 16)])
                                sch.on("gpsimd",
                                       (lambda e, hh=hh, half=half, pair=pair, t0=t0: e.dma_start(
                                           k_scr[seq, 2 * pair + hh, 64:128, t0:t0 + 512],
                                           kstl[hh * 64:(hh + 1) * 64, half * 512:(half + 1) * 512])),
                                       incs=[("s_gp", 16)])
                            for ph in range(3):
                                wt_t = [wqh[pp], wqh[pp], wql[pp]][ph]
                                nr = [normed_h, normed_l, normed_h][ph]
                                for dc in range(NCH):
                                    sch.on("tensor",
                                           (lambda e, dc=dc, half=half, wt_t=wt_t, nr=nr, t0=t0,
                                                   st=(ph == 0 and dc == 0),
                                                   sp=(ph == 2 and dc == NCH - 1): e.matmul(
                                               psD[:, half * 512:(half + 1) * 512],
                                               wt_t[:, dc * 128:(dc + 1) * 128],
                                               nr[dc][:, t0:t0 + 512],
                                               start=st, stop=sp)),
                                           waits=([sch.w("s_dve")] if (ph == 0 and dc == 0) else []),
                                           incs=([("s_mm", 1)] if (ph == 2 and dc == NCH - 1) else []))
                            sch.on("vector",
                                   (lambda e, half=half: e.tensor_copy(
                                       qsth[:, half * 512:(half + 1) * 512],
                                       psD[:, half * 512:(half + 1) * 512])),
                                   waits=[sch.w("s_mm"), sch.w("s_gp")])
                            sch.on("vector",
                                   (lambda e, half=half: e.tensor_tensor(
                                       qstl[:, half * 512:(half + 1) * 512],
                                       psD[:, half * 512:(half + 1) * 512],
                                       qsth[:, half * 512:(half + 1) * 512], ALU.subtract)),
                                   incs=[("s_dve", 1)])
                            for hh in range(2):
                                sch.on("gpsimd",
                                       (lambda e, hh=hh, half=half, pair=pair, t0=t0: e.dma_start(
                                           q_scr[seq, 2 * pair + hh, 0:64, t0:t0 + 512],
                                           qsth[hh * 64:(hh + 1) * 64, half * 512:(half + 1) * 512])),
                                       waits=[sch.w("s_dve")], incs=[("s_gp", 16)])
                                sch.on("gpsimd",
                                       (lambda e, hh=hh, half=half, pair=pair, t0=t0: e.dma_start(
                                           q_scr[seq, 2 * pair + hh, 64:128, t0:t0 + 512],
                                           qstl[hh * 64:(hh + 1) * 64, half * 512:(half + 1) * 512])),
                                       incs=[("s_gp", 16)])
                            for tc in range(4):
                                for ph in range(3):
                                    nr = [normed_h, normed_h, normed_l][ph]
                                    wt_t = [wvh[pp], wvl[pp], wvh[pp]][ph]
                                    for dc in range(NCH):
                                        sch.on("tensor",
                                               (lambda e, dc=dc, tc=tc, half=half, nr=nr, wt_t=wt_t, t0=t0,
                                                       st=(ph == 0 and dc == 0),
                                                       sp=(ph == 2 and dc == NCH - 1): e.matmul(
                                                   psA[:, half * 512 + tc * 128: half * 512 + (tc + 1) * 128],
                                                   nr[dc][:, t0 + tc * 128:t0 + (tc + 1) * 128],
                                                   wt_t[:, dc * 128:(dc + 1) * 128],
                                                   start=st, stop=sp)),
                                               waits=([sch.w("s_dve")] if (tc == 0 and ph == 0 and dc == 0) else []),
                                               incs=([("s_mm", 1)] if (ph == 2 and dc == NCH - 1) else []))
                            sch.on("vector",
                                   (lambda e, half=half: e.tensor_copy(
                                       vst[:, half * 512:(half + 1) * 512],
                                       psA[:, half * 512:(half + 1) * 512])),
                                   waits=[sch.w("s_mm"), sch.w("s_gp")], incs=[("s_dve", 1)])
                            for hh in range(2):
                                sch.on("gpsimd",
                                       (lambda e, hh=hh, half=half, pair=pair, t0=t0: e.dma_start(
                                           bass.AP(v_scr.tensor,
                                                   ((seq * H + 2 * pair + hh) * S + t0) * DK,
                                                   [[DK, 128], [128 * DK, 4], [1, DK]]),
                                           bass.AP(vst, half * 512 + hh * 64,
                                                   [[1024, 128], [128, 4], [1, DK]]))),
                                       waits=[sch.w("s_dve")], incs=[("s_gp", 16)])
                        mm_done_pair.append(sch.count["s_mm"])
                    sch.flush()

            with contextlib.ExitStack() as sc3:
                s3sb = lambda name, shape, dt=F32: sc3.enter_context(nc.sbuf_tensor(f"{name}_s{seq}", shape, dt))
                k_sb = [s3sb(f"k_sb{i}", [128, S], BF16) for i in range(4)]
                q_hd = [s3sb(f"qhd{i}", [128, S], BF16) for i in range(4)]
                q_l = [s3sb(f"ql{i}", [64, S], BF16) for i in range(4)]
                v_sb = [s3sb(f"v_sb{i}", [128, NKC * 130]) for i in range(2)]
                g_sb = [s3sb(f"g_sb{i}", [128, GW], F32R) for i in range(2)]
                p_sb = [s3sb(f"p_sb{i}", [128, 1024]) for i in range(2)]
                wo_sb = [s3sb("wo_sb0", [128, 1024])]
                wo_sb = [wo_sb[0], wo_sb[0]]
                ctx_sb = s3sb("ctx_sb", [128, 2048])
                scw = s3sb("scw", [128, 2048])

                for i in range(2):
                    sch.on("vector", (lambda e, i=i: e.memset(v_sb[i][:, :], 1.0)),
                           incs=[("s_dve", 1)])
                wmemset = sch.w("s_dve")

                def stream_pair(pair, wait_mm):
                    pp = pair % 2
                    for hh in range(2):
                        head = 2 * pair + hh
                        idx = pp * 2 + hh
                        sch.on("sync",
                               (lambda e, idx=idx, head=head: e.dma_start(
                                   k_sb[idx][:, :], k_scr[seq, head, :, :])),
                               waits=wait_mm, incs=[("s_ld", 16)])
                        sch.on("sync",
                               (lambda e, idx=idx, head=head: e.dma_start(
                                   q_hd[idx][0:64, :], q_scr[seq, head, 0:64, :])),
                               incs=[("s_ld", 16)])
                        sch.on("sync",
                               (lambda e, idx=idx, head=head: e.dma_start(
                                   q_hd[idx][64:128, :], q_scr[seq, head, 0:64, :])),
                               incs=[("s_ld", 16)])
                        sch.on("sync",
                               (lambda e, idx=idx, head=head: e.dma_start(
                                   q_l[idx][:, :], q_scr[seq, head, 64:128, :])),
                               incs=[("s_ld", 16)])
                        sch.on("sync",
                               (lambda e, pp=pp, hh=hh, head=head: e.dma_start(
                                   bass.AP(v_sb[pp], hh * 65, [[NKC * 130, 128], [130, NKC], [1, DK]]),
                                   bass.AP(v_scr.tensor, (seq * H + head) * S * DK,
                                           [[DK, 128], [128 * DK, NKC], [1, DK]]))),
                               incs=[("s_ld", 16)])

                def fetch_g(head, wait_mm):
                    slot = head % 2
                    sch.on("gpsimd",
                           (lambda e, slot=slot, head=head: e.dma_start(
                               g_sb[slot][:, :],
                               bass.AP(rbd_hbm.tensor, head * 4096, [[1, 128], [1, GW]]))),
                           waits=wait_mm, incs=[("s_gp", 16)])

                stream_pair(0, [wmemset])
                fetch_g(0, [])
                fetch_g(1, [])
                pair_mm_done = {}
                head_seed_done = {}

                pending_tail = [None, None]

                def make_tail(q0, hh, wpv):
                    dst = (ctx_sb[0:64, q0:q0 + 1024] if hh == 0
                           else scw[0:64, 1024:2048])

                    def tail_a():
                        sch.on("vector", lambda e: e.tensor_copy(
                            scw[64:65, 0:1024], psC[64:65, 0:1024]),
                            waits=[("s_mm", wpv)], incs=[("s_dve", 1)])
                        sch.on("vector",
                               (lambda e, dst=dst: e.tensor_copy(
                                   dst, psC[0:64, 0:1024])),
                               waits=([sch.w("s_gp")] if hh == 1 else []),
                               incs=[("s_dve", 1)])

                    def tail_b():
                        sch.on("vector", lambda e: e.reciprocal(
                            scw[64:65, 1024:2048], scw[64:65, 0:1024]),
                            waits=[sch.w("s_dve")],
                            incs=[("s_dve", 1)])
                        sch.on("tensor", lambda e: e.matmul(
                            psD[0:64, 0:512], ones[64:65, 0:64], scw[64:65, 1024:1536],
                            start=True, stop=True),
                            waits=[sch.w("s_dve")])
                        sch.on("tensor", lambda e: e.matmul(
                            psD[0:64, 512:1024], ones[64:65, 0:64], scw[64:65, 1536:2048],
                            start=True, stop=True), incs=[("s_mm", 1)])
                        sch.on("vector", lambda e: e.tensor_copy(
                            scw[0:64, 0:1024], psD[0:64, 0:1024]),
                            waits=[sch.w("s_mm")], incs=[("s_dve", 1)])
                        sch.on("vector",
                               (lambda e, dst=dst: e.tensor_tensor(
                                   dst, dst, scw[0:64, 0:1024], ALU.mult)),
                               incs=[("s_dve", 1)])
                        if hh == 1:
                            sch.on("gpsimd",
                                   (lambda e, q0=q0: e.dma_start(
                                       ctx_sb[64:128, q0:q0 + 1024], scw[0:64, 1024:2048])),
                                   waits=[sch.w("s_dve")], incs=[("s_gp", 16)])
                    return tail_a, tail_b

                for pair in range(8):
                    pp = pair % 2
                    wkv = sch.w("s_ld")
                    if pair < 7:
                        stream_pair(pair + 1,
                                    [("s_mm", pair_mm_done[pair - 1])] if pair >= 1 else [wmemset])
                    for hh in range(2):
                        head = 2 * pair + hh
                        wg = sch.w("s_gp")
                        exp_hist = []
                        pv_hist = []
                        for qt in range(2):
                            q0 = qt * 1024
                            for kc in range(NKC):
                                bank = [psA, psB][kc % 2]
                                w0a = 2047 + kc * 128 - q0
                                w0b = w0a - 512
                                waits = [wkv, wg]
                                idx = qt * NKC + kc
                                if idx >= 2:
                                    waits.append(("s_act", exp_hist[idx - 2]))
                                else:
                                    waits.append(sch.w("s_act"))
                                sch.on("tensor",
                                       (lambda e, bank=bank, w0a=w0a, gs=g_sb[head % 2]: e.matmul(
                                           bank[:, 0:512], ident_r[:, :],
                                           bass.AP(gs, w0a, [[GW, 128], [-1, 512]]),
                                           start=True, stop=False)),
                                       waits=waits)
                                sch.on("tensor",
                                       (lambda e, bank=bank, w0b=w0b, gs=g_sb[head % 2]: e.matmul(
                                           bank[:, 512:1024], ident_r[:, :],
                                           bass.AP(gs, w0b, [[GW, 128], [-1, 512]]),
                                           start=True, stop=False)))
                                khx = pp * 2 + hh
                                sch.on("tensor",
                                       (lambda e, bank=bank, kc=kc, khx=khx, q0=q0: e.matmul(
                                           bank[:, 0:512],
                                           k_sb[khx][:, kc * 128:(kc + 1) * 128],
                                           q_hd[khx][:, q0:q0 + 512],
                                           start=False, stop=False)))
                                sch.on("tensor",
                                       (lambda e, bank=bank, kc=kc, khx=khx, q0=q0: e.matmul(
                                           bank[:, 0:512],
                                           k_sb[khx][0:64, kc * 128:(kc + 1) * 128],
                                           q_l[khx][:, q0:q0 + 512],
                                           start=False, stop=True)))
                                sch.on("tensor",
                                       (lambda e, bank=bank, kc=kc, khx=khx, q0=q0: e.matmul(
                                           bank[:, 512:1024],
                                           k_sb[khx][:, kc * 128:(kc + 1) * 128],
                                           q_hd[khx][:, q0 + 512:q0 + 1024],
                                           start=False, stop=False)))
                                sch.on("tensor",
                                       (lambda e, bank=bank, kc=kc, khx=khx, q0=q0: e.matmul(
                                           bank[:, 512:1024],
                                           k_sb[khx][0:64, kc * 128:(kc + 1) * 128],
                                           q_l[khx][:, q0 + 512:q0 + 1024],
                                           start=False, stop=True)),
                                       incs=[("s_mm", 1)])
                                if head == 0 and qt == 0 and kc == 0:
                                    head_seed_done[head] = sch.count["s_mm"]
                                w_scores = sch.w("s_mm")
                                if kc == 0 and pending_tail[0] is not None:
                                    pending_tail[0]()
                                    pending_tail[0] = None
                                if kc == 4 and pending_tail[1] is not None:
                                    pending_tail[1]()
                                    pending_tail[1] = None
                                ewaits = [w_scores]
                                if idx >= 2:
                                    ewaits.append(("s_mm", pv_hist[idx - 2]))
                                sch.on("scalar",
                                       (lambda e, bank=bank, kc=kc: e.activation(
                                           p_sb[kc % 2][:, :], bank[:, :], AF.Exp,
                                           bias=negc[:, :], scale=1.0)),
                                       waits=ewaits, incs=[("s_act", 1)])
                                exp_hist.append(sch.count["s_act"])
                                sch.on("tensor",
                                       (lambda e, kc=kc, pp=pp, hh=hh: e.matmul(
                                           psC[0:65, 0:512],
                                           v_sb[pp][:, kc * 130 + hh * 65: kc * 130 + hh * 65 + 65],
                                           p_sb[kc % 2][:, 0:512],
                                           start=(kc == 0), stop=(kc == NKC - 1))),
                                       waits=[sch.w("s_act")] + ([sch.w("s_dve")] if kc == 0 else []))
                                sch.on("tensor",
                                       (lambda e, kc=kc, pp=pp, hh=hh: e.matmul(
                                           psC[0:65, 512:1024],
                                           v_sb[pp][:, kc * 130 + hh * 65: kc * 130 + hh * 65 + 65],
                                           p_sb[kc % 2][:, 512:1024],
                                           start=(kc == 0), stop=(kc == NKC - 1))),
                                       incs=[("s_mm", 1)])
                                pv_hist.append(sch.count["s_mm"])
                            if head == 2 * pair and qt == 0:
                                head_seed_done[head] = sch.count["s_mm"]
                            pending_tail[0], pending_tail[1] = make_tail(
                                q0, hh, sch.count["s_mm"])
                        if head < H - 2:
                            fetch_g(head + 2, [("s_mm", sch.count["s_mm"])])
                    if pending_tail[0] is not None:
                        pending_tail[0]()
                        pending_tail[0] = None
                    if pending_tail[1] is not None:
                        pending_tail[1]()
                        pending_tail[1] = None
                    wctx = [sch.w("s_dve"), sch.w("s_gp")]
                    sch.on("sync",
                           (lambda e, pair=pair: e.dma_start(
                               bass.AP(wo_sb[0], 0, [[1024, 128], [1, 1024]]),
                               wo_in[pair * 128:(pair + 1) * 128, :])),
                           waits=([("s_mm", pair_mm_done[pair - 1])] if pair >= 1 else []),
                           incs=[("s_ld", 16)])
                    wwo = sch.w("s_ld")
                    nreg = 0
                    evac_hist = []
                    for qt in range(2):
                        q0 = qt * 1024
                        for oc in range(NCH):
                            for jh in range(2):
                                reg = nreg % 4
                                tgt = [psD, psB][reg // 2]
                                cols = (reg % 2) * 512
                                waits = wctx + [wwo]
                                if nreg >= 4:
                                    waits.append(("s_dve", evac_hist[nreg - 4]))
                                elif reg // 2 == 1:
                                    waits.append(sch.w("s_act"))
                                sch.on("tensor",
                                       (lambda e, tgt=tgt, cols=cols, pp=pp, oc=oc, q0=q0, jh=jh: e.matmul(
                                           tgt[:, cols:cols + 512],
                                           wo_sb[pp][:, oc * 128:(oc + 1) * 128],
                                           ctx_sb[:, q0 + jh * 512:q0 + (jh + 1) * 512],
                                           start=True, stop=True)),
                                       waits=waits, incs=[("s_mm", 1)])
                                sch.on("vector",
                                       (lambda e, tgt=tgt, cols=cols, oc=oc, q0=q0, jh=jh: e.tensor_tensor(
                                           x1[oc][:, q0 + jh * 512:q0 + (jh + 1) * 512],
                                           x1[oc][:, q0 + jh * 512:q0 + (jh + 1) * 512],
                                           tgt[:, cols:cols + 512], ALU.add)),
                                       waits=[sch.w("s_mm")], incs=[("s_dve", 1)])
                                evac_hist.append(sch.count["s_dve"])
                                nreg += 1
                    pair_mm_done[pair] = sch.count["s_mm"]
                sch.flush()

            sc4 = contextlib.ExitStack()
            tmp0 = sc4.enter_context(nc.sbuf_tensor(f"tmp0b_s{seq}", [128, 512], F32))
            xsq = [sc4.enter_context(nc.sbuf_tensor(f"xsqb_{i}_s{seq}", [128, 512], F32))
                   for i in range(2)]
            for tt in range(4):
                t0 = tt * 512
                for dc in range(NCH):
                    sch.on("tensor",
                           (lambda e, dc=dc, t0=t0: e.matmul(
                               psA[0:1, 0:512], ones[:, 0:1], x1[dc][:, t0:t0 + 512],
                               start=(dc == 0), stop=(dc == NCH - 1))),
                           waits=[sch.w("s_dve")] if dc == 0 else [],
                           incs=([("s_mm", 1)] if dc == NCH - 1 else []))
                for dc in range(NCH):
                    sch.on("scalar",
                           (lambda e, dc=dc, t0=t0: e.activation(
                               xsq[dc % 2][:, :], x1[dc][:, t0:t0 + 512], AF.Square)),
                           waits=[sch.w("s_mm")], incs=[("s_act", 1)])
                    sch.on("tensor",
                           (lambda e, dc=dc, st=(dc == 0): e.matmul(
                               psA[0:1, 512:1024], ones[:, 0:1], xsq[dc % 2][:, :],
                               start=st, stop=(dc == NCH - 1))),
                           waits=[sch.w("s_act")], incs=[("s_mm", 1)])
                wmm = sch.w("s_mm")
                sch.on("vector", lambda e: e.tensor_scalar(
                    rows[0:1, 0:512], psA[0:1, 0:512], 1.0 / D, None, ALU.mult), waits=[wmm])
                sch.on("vector", lambda e: e.tensor_scalar(
                    rows[0:1, 512:1024], psA[0:1, 512:1024], 1.0 / D, None, ALU.mult))
                sch.on("vector", lambda e: e.tensor_tensor(
                    rows[0:1, 1024:1536], rows[0:1, 0:512], rows[0:1, 0:512], ALU.mult))
                sch.on("vector", lambda e: e.tensor_tensor(
                    rows[0:1, 512:1024], rows[0:1, 512:1024], rows[0:1, 1024:1536], ALU.subtract))
                sch.on("vector", lambda e: e.tensor_scalar(
                    rows[0:1, 512:1024], rows[0:1, 512:1024], 1e-5, None, ALU.add),
                    incs=[("s_dve", 1)])
                sch.on("scalar", lambda e: e.activation(
                    rows[0:1, 1024:1536], rows[0:1, 512:1024], AF.Sqrt),
                    waits=[sch.w("s_dve")], incs=[("s_act", 1)])
                sch.on("vector", lambda e: e.reciprocal(
                    rows[0:1, 512:1024], rows[0:1, 1024:1536]),
                    waits=[sch.w("s_act")], incs=[("s_dve", 1)])
                sch.on("tensor", lambda e: e.matmul(
                    psB[:, 0:512], ones[0:1, 0:128], rows[0:1, 0:512],
                    start=True, stop=True), waits=[sch.w("s_dve")])
                sch.on("tensor", lambda e: e.matmul(
                    psB[:, 512:1024], ones[0:1, 0:128], rows[0:1, 512:1024],
                    start=True, stop=True), incs=[("s_mm", 1)])
                wmm = sch.w("s_mm")
                for dc in range(NCH):
                    sch.on("vector",
                           (lambda e, dc=dc, t0=t0: e.tensor_tensor(
                               tmp0[:, :], x1[dc][:, t0:t0 + 512], psB[:, 0:512], ALU.subtract)),
                           waits=[wmm] if dc == 0 else [])
                    sch.on("vector",
                           (lambda e, dc=dc, t0=t0: e.tensor_tensor(
                               x1[dc][:, t0:t0 + 512], tmp0[:, :], psB[:, 512:1024], ALU.mult)),
                           incs=[("s_dve", 1)])
            sch.flush()
            sc4.close()

            with contextlib.ExitStack() as sc5:
                s5sb = lambda name, shape: sc5.enter_context(nc.sbuf_tensor(f"{name}_s{seq}", shape, F32))
                base_sb = s5sb("base_sb", [128, NCH * 1024])
                x2 = [s5sb(f"x2_{c}", [128, 1024]) for c in range(NCH)]
                gexp = s5sb("gexp", [128, 80])
                gsum = s5sb("gsum", [128, 8])
                gm1 = s5sb("gm1", [128, 8])
                gm2 = s5sb("gm2", [128, 8])
                gm3 = s5sb("gm3", [128, 8])
                grc = s5sb("grc", [128, 8])
                gt1 = s5sb("gt1", [128, 80])
                gt2 = s5sb("gt2", [128, 80])
                gsel = s5sb("gsel", [128, 80])
                gew = s5sb("gew", [128, 64])
                ewt = s5sb("ewt", [8, 1024])
                ewfm = s5sb("ewfm", [128, 1024])
                zw = s5sb("zw", [128, 1024])
                xtm2 = s5sb("xtm2", [128, 1024])
                sqj = s5sb("sqj", [128, 1024])
                lrow = [s5sb(f"lrow{i}", [128, 1]) for i in range(6)]
                otile = [s5sb(f"otile{i}", [128, 1024]) for i in range(2)]

                sch.on("sync",
                       (lambda e: e.dma_start(
                           bass.AP(base_sb, 0, [[NCH * 1024, 128], [1024, NCH], [1, 1024]]),
                           bass.AP(bw_in.tensor, 0, [[D, 128], [128 * D, NCH], [1, 1024]]))),
                       incs=[("s_ld", 16)])
                wbase = sch.w("s_ld")

                for tt in range(2):
                    t0 = tt * 1024
                    for tc in range(8):
                        for dc in range(NCH):
                            sch.on("tensor",
                                   (lambda e, tc=tc, dc=dc, t0=t0: e.matmul(
                                       psA[:, tc * 16:tc * 16 + 10],
                                       x1[dc][:, t0 + tc * 128:t0 + (tc + 1) * 128],
                                       gate_sb[:, dc * 10:(dc + 1) * 10],
                                       start=(dc == 0), stop=(dc == NCH - 1))),
                                   waits=([sch.w("s_dve")] if (tc == 0 and dc == 0) else []),
                                   incs=([("s_mm", 1)] if dc == NCH - 1 else []))
                    wg = sch.w("s_mm")
                    for tc in range(8):
                        sch.on("scalar",
                               (lambda e, tc=tc: e.activation(
                                   gexp[:, tc * 10:(tc + 1) * 10], psA[:, tc * 16:tc * 16 + 10],
                                   AF.Exp, accum_out=gsum[:, tc:tc + 1])),
                               waits=[wg], incs=[("s_act", 1)])
                    wexp = sch.w("s_act")
                    for tc in range(8):
                        sch.on("vector",
                               (lambda e, tc=tc: e.reciprocal(
                                   grc[:, tc:tc + 1], gsum[:, tc:tc + 1])),
                               waits=[wexp] if tc == 0 else [],
                               incs=([("s_dve", 1)] if tc == 7 else []))
                    wb = sch.w("s_dve")
                    for tc in range(8):
                        sch.on("vector",
                               (lambda e, tc=tc: e.tensor_tensor(
                                   gt1[:, tc * 10:(tc + 1) * 10], gexp[:, tc * 10:(tc + 1) * 10],
                                   grc[:, tc:tc + 1].broadcast_to([128, 10]), ALU.mult)),
                               waits=[wb] if tc == 0 else [])
                    for tc in range(8):
                        sch.on("vector",
                               (lambda e, tc=tc: e.reduce_max(
                                   gm1[:, tc:tc + 1], gt1[:, tc * 10:(tc + 1) * 10], axis=AX.X)),
                               incs=([("s_dve", 1)] if tc == 7 else []))
                    wb = sch.w("s_dve")
                    for tc in range(8):
                        sch.on("vector",
                               (lambda e, tc=tc: e.tensor_tensor(
                                   gsel[:, tc * 10:(tc + 1) * 10], gt1[:, tc * 10:(tc + 1) * 10],
                                   gm1[:, tc:tc + 1].broadcast_to([128, 10]), ALU.is_equal)),
                               waits=[wb] if tc == 0 else [],
                               incs=([("s_dve", 1)] if tc == 7 else []))
                    sch.on("vector", lambda e: e.tensor_scalar(
                        gt2[:, :], gsel[:, :], -1e9, None, ALU.mult),
                        waits=[sch.w("s_dve")], incs=[("s_dve", 1)])
                    sch.on("vector", lambda e: e.tensor_tensor(
                        gt2[:, :], gt1[:, :], gt2[:, :], ALU.add),
                        waits=[sch.w("s_dve")], incs=[("s_dve", 1)])
                    for tc in range(8):
                        sch.on("vector",
                               (lambda e, tc=tc: e.reduce_max(
                                   gm2[:, tc:tc + 1], gt2[:, tc * 10:(tc + 1) * 10], axis=AX.X)),
                               waits=[sch.w("s_dve")] if tc == 0 else [],
                               incs=([("s_dve", 1)] if tc == 7 else []))
                    wb = sch.w("s_dve")
                    for tc in range(8):
                        sch.on("vector",
                               (lambda e, tc=tc: e.tensor_tensor(
                                   gt2[:, tc * 10:(tc + 1) * 10], gt2[:, tc * 10:(tc + 1) * 10],
                                   gm2[:, tc:tc + 1].broadcast_to([128, 10]), ALU.is_equal)),
                               waits=[wb] if tc == 0 else [],
                               incs=([("s_dve", 1)] if tc == 7 else []))
                    sch.on("vector", lambda e: e.tensor_tensor(
                        gsel[:, :], gsel[:, :], gt2[:, :], ALU.add),
                        waits=[sch.w("s_dve")], incs=[("s_dve", 1)])
                    for tc in range(8):
                        sch.on("vector",
                               (lambda e, tc=tc: e.tensor_tensor(
                                   gew[:, tc * 8:(tc + 1) * 8], gt1[:, tc * 10:tc * 10 + 8],
                                   gsel[:, tc * 10:tc * 10 + 8], ALU.mult)),
                               waits=[sch.w("s_dve")] if tc == 0 else [],
                               incs=([("s_dve", 1)] if tc == 7 else []))
                    for tc in range(8):
                        sch.on("vector",
                               (lambda e, tc=tc: e.reduce_sum(
                                   gm2[:, tc:tc + 1], gew[:, tc * 8:(tc + 1) * 8], axis=AX.X)),
                               waits=[sch.w("s_dve")] if tc == 0 else [],
                               incs=([("s_dve", 1)] if tc == 7 else []))
                    sch.on("vector", lambda e: e.tensor_scalar(
                        gm3[:, :], gm2[:, :], 0.0, None, ALU.is_equal),
                        waits=[sch.w("s_dve")], incs=[("s_dve", 1)])
                    sch.on("vector", lambda e: e.tensor_tensor(
                        gm2[:, :], gm2[:, :], gm3[:, :], ALU.add),
                        waits=[sch.w("s_dve")], incs=[("s_dve", 1)])
                    wdz = sch.w("s_dve")
                    for tc in range(8):
                        sch.on("vector",
                               (lambda e, tc=tc: e.reciprocal(
                                   grc[:, tc:tc + 1], gm2[:, tc:tc + 1])),
                               waits=[wdz] if tc == 0 else [],
                               incs=([("s_dve", 1)] if tc == 7 else []))
                    sch.on("vector", lambda e: e.tensor_scalar(
                        grc[:, :], grc[:, :], 2.0, None, ALU.mult),
                        waits=[sch.w("s_dve")], incs=[("s_dve", 1)])
                    wb = sch.w("s_dve")
                    for tc in range(8):
                        sch.on("vector",
                               (lambda e, tc=tc: e.tensor_tensor(
                                   gew[:, tc * 8:(tc + 1) * 8], gew[:, tc * 8:(tc + 1) * 8],
                                   grc[:, tc:tc + 1].broadcast_to([128, 8]), ALU.mult)),
                               waits=[wb] if tc == 0 else [],
                               incs=([("s_dve", 1)] if tc == 7 else []))
                    for tc in range(8):
                        sch.on("tensor",
                               (lambda e, tc=tc: e.transpose(
                                   psB[0:8, tc * 128:(tc + 1) * 128],
                                   gew[:, tc * 8:(tc + 1) * 8], ident[:, :])),
                               waits=[sch.w("s_dve")] if tc == 0 else [],
                               incs=[("s_mm", 1)])
                        sch.on("vector",
                               (lambda e, tc=tc: e.tensor_copy(
                                   ewt[0:8, tc * 128:(tc + 1) * 128],
                                   psB[0:8, tc * 128:(tc + 1) * 128])),
                               waits=[sch.w("s_mm")], incs=[("s_dve", 1)])
                        sch.on("tensor",
                               (lambda e, tc=tc: e.matmul(
                                   psC[:, tc * 128:(tc + 1) * 128], rbc_sb[:, :],
                                   ewt[0:8, tc * 128:(tc + 1) * 128], start=True, stop=True)),
                               waits=[sch.w("s_dve")], incs=[("s_mm", 1)])
                    sch.on("vector", lambda e: e.tensor_copy(ewfm[:, :], psC[:, :]),
                           waits=[sch.w("s_mm")], incs=[("s_dve", 1)])
                    for jh in range(2):
                        for dc in range(NCH):
                            sch.on("tensor",
                                   (lambda e, jh=jh, dc=dc, t0=t0: e.matmul(
                                       psD[:, jh * 512:(jh + 1) * 512],
                                       la_sb[:, dc * 128:(dc + 1) * 128],
                                       x1[dc][:, t0 + jh * 512:t0 + jh * 512 + 512],
                                       start=(dc == 0), stop=(dc == NCH - 1))),
                                   waits=([sch.w("s_dve")] if (jh == 0 and dc == 0) else []),
                                   incs=([("s_mm", 1)] if dc == NCH - 1 else []))
                    sch.on("vector", lambda e: e.tensor_tensor(
                        zw[:, :], psD[:, :], ewfm[:, :], ALU.mult),
                        waits=[sch.w("s_mm")], incs=[("s_dve", 1)])
                    wzw = sch.w("s_dve")
                    evh = []
                    for oc in range(NCH):
                        tgt = [psA, psB][oc % 2]
                        waits = [wbase, wzw]
                        if oc >= 2:
                            waits.append(("s_dve", evh[oc - 2]))
                        for jh in range(2):
                            for dc in range(NCH):
                                sch.on("tensor",
                                       (lambda e, oc=oc, jh=jh, dc=dc, t0=t0, tgt=tgt: e.matmul(
                                           tgt[:, jh * 512:(jh + 1) * 512],
                                           base_sb[:, dc * 1024 + oc * 128:dc * 1024 + (oc + 1) * 128],
                                           x1[dc][:, t0 + jh * 512:t0 + jh * 512 + 512],
                                           start=(dc == 0), stop=False)),
                                       waits=(waits if (jh == 0 and dc == 0) else []))
                            sch.on("tensor",
                                   (lambda e, oc=oc, jh=jh, tgt=tgt: e.matmul(
                                       tgt[:, jh * 512:(jh + 1) * 512],
                                       lb_sb[:, oc * 128:(oc + 1) * 128],
                                       zw[:, jh * 512:(jh + 1) * 512],
                                       start=False, stop=True)),
                                   incs=[("s_mm", 1)])
                        sch.on("vector",
                               (lambda e, oc=oc, tgt=tgt: e.tensor_scalar(
                                   x2[oc][:, :], tgt[:, :], bbc[:, oc:oc + 1], None, ALU.add)),
                               waits=[sch.w("s_mm")])
                        sch.on("vector",
                               (lambda e, oc=oc, t0=t0: e.tensor_tensor(
                                   x2[oc][:, :], x2[oc][:, :], x1[oc][:, t0:t0 + 1024], ALU.add)),
                               incs=[("s_dve", 1)])
                        evh.append(sch.count["s_dve"])
                    wffn = sch.w("s_dve")
                    for tc in range(8):
                        tgt = [psD, psC][tc % 2]
                        for dc in range(NCH):
                            sch.on("tensor",
                                   (lambda e, tc=tc, dc=dc, tgt=tgt: e.transpose(
                                       tgt[:, dc * 128:(dc + 1) * 128],
                                       x2[dc][:, tc * 128:(tc + 1) * 128], ident[:, :])),
                                   waits=([wffn, sch.w("s_dve")] if dc == 0 else []),
                                   incs=([("s_mm", 1)] if dc == NCH - 1 else []))
                        sch.on("vector",
                               (lambda e, tgt=tgt: e.tensor_copy(xtm2[:, :], tgt[:, :])),
                               waits=[sch.w("s_mm")])
                        sch.on("vector", lambda e: e.reduce_sum(
                            lrow[0][:, :], xtm2[:, :], axis=AX.X), incs=[("s_dve", 1)])
                        sch.on("scalar", lambda e: e.activation(
                            sqj[:, :], xtm2[:, :], AF.Square, accum_out=lrow[1][:, :]),
                            waits=[sch.w("s_dve")], incs=[("s_act", 1)])
                        sch.on("vector", lambda e: e.tensor_scalar(
                            lrow[2][:, :], lrow[0][:, :], 1.0 / D, None, ALU.mult),
                            waits=[sch.w("s_act")])
                        sch.on("vector", lambda e: e.tensor_scalar(
                            lrow[3][:, :], lrow[1][:, :], 1.0 / D, None, ALU.mult),
                            incs=[("s_dve", 1)])
                        sch.on("vector", lambda e: e.tensor_tensor(
                            lrow[4][:, :], lrow[2][:, :], lrow[2][:, :], ALU.mult),
                            waits=[sch.w("s_dve")], incs=[("s_dve", 1)])
                        sch.on("vector", lambda e: e.tensor_tensor(
                            lrow[4][:, :], lrow[3][:, :], lrow[4][:, :], ALU.subtract),
                            waits=[sch.w("s_dve")], incs=[("s_dve", 1)])
                        sch.on("vector", lambda e: e.tensor_scalar(
                            lrow[4][:, :], lrow[4][:, :], 1e-5, None, ALU.add),
                            waits=[sch.w("s_dve")], incs=[("s_dve", 1)])
                        sch.on("scalar", lambda e: e.activation(
                            lrow[5][:, :], lrow[4][:, :], AF.Sqrt),
                            waits=[sch.w("s_dve")], incs=[("s_act", 1)])
                        sch.on("vector", lambda e: e.reciprocal(lrow[4][:, :], lrow[5][:, :]),
                               waits=[sch.w("s_act")], incs=[("s_dve", 1)])
                        ob = otile[tc % 2]
                        sch.on("vector",
                               (lambda e, ob=ob: e.tensor_tensor(
                                   ob[:, :], xtm2[:, :],
                                   lrow[2][:, :].broadcast_to([128, 1024]), ALU.subtract)),
                               waits=[("s_out", max(sch.flush_base["s_out"],
                                                    sch.count["s_out"] - 16)),
                                      sch.w("s_dve")])
                        sch.on("vector",
                               (lambda e, ob=ob: e.tensor_tensor(
                                   ob[:, :], ob[:, :],
                                   lrow[4][:, :].broadcast_to([128, 1024]), ALU.mult)),
                               incs=[("s_dve", 1)])
                        sch.on("sync",
                               (lambda e, ob=ob, tc=tc, t0=t0: e.dma_start(
                                   y_out[seq, t0 + tc * 128:t0 + (tc + 1) * 128, :], ob[:, :])),
                               waits=[sch.w("s_dve")], incs=[("s_out", 16)])
                sch.flush()

        sch.on("gpsimd", lambda e: e.nop(), waits=[sch.w("s_out")])
        sch.flush()

    return nc


_NC_CACHE = None


def kernel(**inputs):
    global _NC_CACHE
    if _NC_CACHE is None:
        _NC_CACHE = build()
    nc = _NC_CACHE

    x = np.ascontiguousarray(inputs["L0_attn_hidden_states"], dtype=np.float32)
    base = {
        "wq": np.ascontiguousarray(inputs["L0_attn_Wq"], np.float32),
        "wk": np.ascontiguousarray(inputs["L0_attn_Wk"], np.float32),
        "wv": np.ascontiguousarray(inputs["L0_attn_Wv"], np.float32),
        "wo": np.ascontiguousarray(inputs["L0_attn_Wo"], np.float32),
        "ln_g": np.ascontiguousarray(inputs["L0_attn_ln_g"], np.float32),
        "ln_b": np.ascontiguousarray(inputs["L0_attn_ln_b"], np.float32),
        "rel_bias": np.ascontiguousarray(inputs["L0_attn_rel_bias"], np.float32),
        "base_wt": np.ascontiguousarray(np.asarray(inputs["L0_ffn_base_w"], np.float32).T),
        "base_b": np.ascontiguousarray(inputs["L0_ffn_base_b"], np.float32),
        "gate_wt": np.ascontiguousarray(np.asarray(inputs["L0_ffn_gate_w"], np.float32).T),
        "lora_at": np.ascontiguousarray(
            np.asarray(inputs["L0_ffn_lora_A"], np.float32).transpose(2, 0, 1).reshape(D, 128)),
        "lora_bt": np.ascontiguousarray(
            np.asarray(inputs["L0_ffn_lora_B"], np.float32).transpose(0, 2, 1).reshape(128, D)),
        "onehot": _bucket_onehot(),
        "ident": np.eye(128, dtype=np.float32),
        "rbc": _rbc_const(),
    }
    in_maps = [dict(base, x=np.ascontiguousarray(x[c * BPC:(c + 1) * BPC]))
               for c in range(NCORES)]
    res = run_bass_kernel_spmd(nc, in_maps, core_ids=list(range(NCORES)))
    return np.concatenate([r["y"] for r in res.results], axis=0)
